# version 06
# baseline (speedup 1.0000x reference)
import numpy as np
import ml_dtypes
from contextlib import ExitStack

import concourse.bass as bass
import concourse.mybir as mybir
from concourse import library_config
from concourse.bass_utils import run_bass_kernel_spmd
from concourse.alu_op_type import AluOpType as OP

F32 = mybir.dt.float32
BF16 = mybir.dt.bfloat16
FP8 = mybir.dt.float8e4
AF = mybir.ActivationFunctionType
AX = mybir.AxisListType.X

S, D = 2048, 1024
TT = S // 128
FC = D // 128
N_SUB = 8
SUB_MASK = [1, 1, 1, 1, 1, 1, 1, 1]

ENGS = ("tensor", "vector", "scalar", "gpsimd", "sync")
SEMN = {"tensor": "T", "vector": "V", "scalar": "S", "gpsimd": "G", "sync": "Y"}
AUTO_OPS = {
    "matmul", "transpose", "tensor_copy", "tensor_scalar", "tensor_tensor",
    "scalar_tensor_tensor", "tensor_tensor_scan", "reduce_sum", "reduce_max",
    "tensor_reduce", "reciprocal", "memset", "activation", "copy", "add", "mul",
    "partition_broadcast", "iota", "tensor_add", "tensor_mul", "tensor_sub",
    "tensor_max", "select",
}


class EP:

    def __init__(self, k, eng, sem):
        self._k, self._e, self._s = k, eng, sem

    def __getattr__(self, n):
        f = getattr(self._e, n)
        if n in AUTO_OPS:
            k, s = self._k, self._s

            def g(*a, **kw):
                return k.inc(f(*a, **kw), s)
            return g
        return f

    def wait(self, sem, val=None):
        k = self._k
        self._e.wait_ge(k.sems[sem], k.c[sem] if val is None else val)


class K:
    def __init__(self, nc, blk, stack):
        self.nc, self.blk = nc, blk
        self.sems, self.c = {}, {}
        for n in ("T", "V", "S", "G", "Y", "dY", "dY0", "dY1", "dA0", "dA1"):
            self.sems[n] = stack.enter_context(nc.semaphore(n))
            self.c[n] = 0

    def inc(self, instr, s, n=None):
        n = (16 if s.startswith("d") else 1) if n is None else n
        instr.then_inc(self.sems[s], n)
        self.c[s] += n
        return instr

    def on(self, eng, fn):
        sem = SEMN[eng]
        getattr(self.blk, eng)(lambda e: fn(EP(self, e, sem)))

    def barrier(self):
        for e in ENGS:
            sn = SEMN[e]

            def mk(sn):
                def f(ep):
                    ep._e.maybe_drain_then_inc((self.sems[sn], 1))
                    self.c[sn] += 1
                return f
            self.on(e, mk(sn))
        targets = dict(self.c)
        for e in ENGS:
            sn = SEMN[e]

            def mk2(sn):
                def f(ep):
                    for o, v in targets.items():
                        if o != sn and v > 0:
                            ep._e.wait_ge(self.sems[o], v)
                return f
            self.on(e, mk2(sn))


def build_program():
    nc = bass.Bass("TRN2", target_bir_lowering=False)
    I = {}
    _ns = [0]
    _sb0, _ps0 = nc.sbuf_tensor, nc.psum_tensor

    def _sb(name, *a, **kw):
        _ns[0] += 1
        return _sb0("%s_%d" % (name, _ns[0]), *a, **kw)

    def _ps(name, *a, **kw):
        _ns[0] += 1
        return _ps0("%s_%d" % (name, _ns[0]), *a, **kw)
    nc.sbuf_tensor = _sb
    nc.psum_tensor = _ps

    def din(name, shape, dtype=F32):
        I[name] = nc.dram_tensor(name, list(shape), dtype, kind="ExternalInput").ap()

    din("x", (S, D))
    for w in ("aaWq", "aaWk", "aaWv", "aaWo"):
        din(w, (D, D))
    din("aab", (4, D))
    for w in ("abWq", "abWk", "abWv", "abWo"):
        din(w, (D, D))
    din("abWoPk", (64, 16 * 1024))
    din("abb", (4, D))
    din("f0k", (D, D), BF16)
    din("f0scales", (1, 2))
    din("f0bias", (1, D), BF16)
    din("ident", (128, 128))
    din("ropeAc", (S, 512)); din("ropeAs", (S, 512))
    din("ropeBc", (S, 512)); din("ropeBs", (S, 512))
    din("upT", (D, 4096))
    din("convw", (2048, 4)); din("convb", (2048, 1))
    din("bdq", (16, 128, 128)); din("bdk", (16, 128, 128)); din("bdv", (16, 128, 128))
    din("wigfgT", (6144, 8)); din("bigfg", (4, 2))
    din("lnw", (1, 2048)); din("skipv", (1, 2048))
    din("downT", (2048, D))
    din("l1Wq", (16, 64, 64)); din("l1Wp", (D, D)); din("l1bp", (1, D))
    din("l1WqPk", (128, 512)); din("l1WpPk", (64, 16 * 1024))
    din("w1T", (D, 4096)); din("w1b", (1, 4096))
    din("w2T", (4096, D)); din("w2b", (1, D))
    din("tril", (4, 128, 512))

    out = nc.dram_tensor("out", [S, D], F32, kind="ExternalOutput").ap()
    d_scr = nc.dram_tensor("d_scr", [1, 65536], F32, kind="Internal").ap()
    d_park = nc.dram_tensor("d_park", [S, D], F32, kind="Internal").ap()
    d_z = nc.dram_tensor("d_z", [S, 2048], BF16, kind="Internal").ap()
    d_w1q = nc.dram_tensor("d_w1q", [D, 4096], BF16, kind="Internal").ap()
    d_w2q = nc.dram_tensor("d_w2q", [4096, D], BF16, kind="Internal").ap()
    d_h = nc.dram_tensor("d_h", [S, 4096], BF16, kind="Internal").ap()
    d_m = nc.dram_tensor("d_m", [16, 2048], BF16, kind="Internal").ap()
    d_q = nc.dram_tensor("d_q", [2048, S], BF16, kind="Internal").ap()
    d_k = nc.dram_tensor("d_k", [2048, S], BF16, kind="Internal").ap()
    d_vt = nc.dram_tensor("d_vt", [S, 2048], BF16, kind="Internal").ap()
    d_xct = nc.dram_tensor("d_xct", [S, 2048], BF16, kind="Internal").ap()

    stack = ExitStack()
    with stack:
        sbuf = lambda name, shape, dtype: stack.enter_context(nc.sbuf_tensor(name, list(shape), dtype))
        RH = {}

        def alloc_R():
            RH["g"] = nc.sbuf_tensor("R", [128, TT * 1024], F32, side="right")
            RH["t"] = RH["g"].__enter__()
            RH["ap"] = RH["t"][:, :]

        def free_R():
            RH["g"].__exit__(None, None, None)
        alloc_R()
        R = RH["ap"]
        xlnT_t = sbuf("xlnT", [128, FC * S], BF16)
        stats = sbuf("stats", [128, 64], F32)
        small = sbuf("small", [128, 16], F32)
        identb = sbuf("identb", [128, 128], BF16)
        onesb = sbuf("onesb", [1, 512], BF16)
        onesf = sbuf("onesf", [1, 128], F32)
        xtile = sbuf("xtile", [128, 2048], BF16)
        idstage = sbuf("idstage", [128, 128], F32)
        xlnT = xlnT_t[:, :]

        def xlnT_grid(i):
            return xlnT_t.ap().rearrange("p (c s) -> p c s", c=FC)[:, :, i * 128:(i + 1) * 128]

        blk = stack.enter_context(nc.Block())
        k = K(nc, blk, stack)

        def y0(y):
            for i in range(TT):
                k.inc(y.dma_start(out=R[:, i * 1024:(i + 1) * 1024], in_=I["x"][i * 128:(i + 1) * 128, :]), "dY")
            k.inc(y.dma_start(out=idstage[:, :], in_=I["ident"][:, :]), "dY")
        k.on("sync", y0)

        def g1(g):
            g.wait("dY")
            g.tensor_copy(identb[:, :], idstage[:, :])
        k.on("gpsimd", g1)

        def v0(v):
            v.memset(onesb[:, :], 1.0)
            v.memset(onesf[:, :], 1.0)
            v.memset(small[:, 7:8], 1e-5)
        k.on("vector", v0)
        k.barrier()

        def load_cast(stage, specs, per=2048, after=None):
            items = []
            for (src, dstfn, rows, cols) in specs:
                for c in range((rows + 127) // 128):
                    items.append((src, dstfn, c, min(128, rows - c * 128), cols))
            gcnt, dcnt = [], []
            g0c = k.c["G"]
            for j, (src, dstfn, c, r, cols) in enumerate(items):
                def yb(y, j=j, src=src, c=c, r=r, cols=cols):
                    if j == 0 and after is not None:
                        y.wait(after[0], after[1])
                    if j >= 2:
                        y.wait("G", gcnt[j - 2])
                    else:
                        y.wait("G", g0c)
                    k.inc(y.dma_start(out=stage[0:r, (j % 2) * per:(j % 2) * per + cols],
                                      in_=src[c * 128:c * 128 + r, :]), "dY%d" % (j % 2))
                    dcnt.append(k.c["dY%d" % (j % 2)])
                k.on("sync", yb)

                def gb(g, j=j, dstfn=dstfn, c=c, r=r, cols=cols):
                    if j == 0 and after is not None:
                        g.wait(after[0], after[1])
                    g.wait("dY%d" % (j % 2), dcnt[j])
                    g.tensor_copy(dstfn(c), stage[0:r, (j % 2) * per:(j % 2) * per + cols])
                    gcnt.append(k.c["G"])
                k.on("gpsimd", gb)

        def load_rows_cast(stage, specs, per=2048, after=None):
            load_cast(stage, [(src, (lambda dst: lambda c: dst)(dst), 1, n) for (src, dst, n) in specs], per, after)

        def emit_ln_stats():
            def _sb1(s):
                sq0 = k.c["S"]
                for i in range(TT):
                    if i >= 1:
                        s.wait("S", sq0 + i)
                    s.activation(out=xtile[:, (i % 2) * 1024:(i % 2) * 1024 + 1024],
                                 in_=R[:, i * 1024:(i + 1) * 1024],
                                 func=AF.Square, accum_out=stats[:, 16 + i:17 + i])
            k.on("scalar", _sb1)

            def _vb1(v):
                for i in range(TT):
                    v.reduce_sum(stats[:, i:i + 1], R[:, i * 1024:(i + 1) * 1024], axis=AX)
                v.wait("S")
                v.wait("V")
                v.tensor_scalar(stats[:, 0:16], stats[:, 0:16], 1.0 / D, None, OP.mult)
                v.tensor_scalar(stats[:, 16:32], stats[:, 16:32], 1.0 / D, None, OP.mult)
                v.wait("V")
                v.tensor_tensor(stats[:, 48:64], stats[:, 0:16], stats[:, 0:16], OP.mult)
                v.wait("V")
                v.tensor_tensor(stats[:, 16:32], stats[:, 16:32], stats[:, 48:64], OP.subtract)
            k.on("vector", _vb1)

            def _sb2(s):
                s.wait("V")
                s.activation(out=stats[:, 16:32], in_=stats[:, 16:32], func=AF.Sqrt, bias=small[:, 7:8])
            k.on("scalar", _sb2)

            def _vb2(v):
                v.wait("S")
                v.reciprocal(stats[:, 32:48], stats[:, 16:32])
                v.wait("V")
                v.tensor_tensor(stats[:, 48:64], stats[:, 0:16], stats[:, 32:48], OP.mult)
                v.wait("V")
                v.tensor_scalar(stats[:, 48:64], stats[:, 48:64], -1.0, None, OP.mult)
            k.on("vector", _vb2)

        def emit_ln(ps_bf):
            def sb1(s):
                for i in range(TT):
                    if i >= 1:
                        s.wait("S")
                    s.activation(out=xtile[:, (i % 2) * 1024:(i % 2) * 1024 + 1024],
                                 in_=R[:, i * 1024:(i + 1) * 1024],
                                 func=AF.Square, accum_out=stats[:, 16 + i:17 + i])
            k.on("scalar", sb1)

            def vb1(v):
                for i in range(TT):
                    v.reduce_sum(stats[:, i:i + 1], R[:, i * 1024:(i + 1) * 1024], axis=AX)
                v.wait("S")
                v.wait("V")
                v.tensor_scalar(stats[:, 0:16], stats[:, 0:16], 1.0 / D, None, OP.mult)
                v.tensor_scalar(stats[:, 16:32], stats[:, 16:32], 1.0 / D, None, OP.mult)
                v.wait("V")
                v.tensor_tensor(stats[:, 48:64], stats[:, 0:16], stats[:, 0:16], OP.mult)
                v.wait("V")
                v.tensor_tensor(stats[:, 16:32], stats[:, 16:32], stats[:, 48:64], OP.subtract)
            k.on("vector", vb1)

            def sb2(s):
                s.wait("V")
                s.activation(out=stats[:, 16:32], in_=stats[:, 16:32], func=AF.Sqrt, bias=small[:, 7:8])
            k.on("scalar", sb2)

            def vb2(v):
                v.wait("S")
                v.reciprocal(stats[:, 32:48], stats[:, 16:32])
                v.wait("V")
                v.tensor_tensor(stats[:, 48:64], stats[:, 0:16], stats[:, 32:48], OP.mult)
                v.wait("V")
                v.tensor_scalar(stats[:, 48:64], stats[:, 48:64], -1.0, None, OP.mult)
            k.on("vector", vb2)

            scnt, tcnt, vcnt = [], [], []
            vready = k.c["V"]
            for i in range(TT):
                def sb3(s, i=i):
                    if i == 0:
                        s.wait("V", vready)
                    if i >= 2:
                        s.wait("T", tcnt[i - 2])
                    s.activation(out=xtile[:, (i % 2) * 1024:(i % 2) * 1024 + 1024],
                                 in_=R[:, i * 1024:(i + 1) * 1024], func=AF.Identity,
                                 scale=stats[:, 32 + i:33 + i], bias=stats[:, 48 + i:49 + i])
                    scnt.append(k.c["S"])
                k.on("scalar", sb3)

                def tb(t, i=i):
                    t.wait("S", scnt[i])
                    if i >= 2:
                        t.wait("V", vcnt[i - 2])
                    for c in range(FC):
                        t.transpose(ps_bf[:, (i % 2) * 1024 + c * 128:(i % 2) * 1024 + c * 128 + 128],
                                    xtile[:, (i % 2) * 1024 + c * 128:(i % 2) * 1024 + c * 128 + 128],
                                    identb[:, :])
                    tcnt.append(k.c["T"])
                k.on("tensor", tb)

                def vb3(v, i=i):
                    v.wait("T", tcnt[i])
                    src = ps_bf[:, (i % 2) * 1024:(i % 2) * 1024 + 1024].rearrange("p (c s) -> p c s", c=FC)
                    v.tensor_copy(xlnT_grid(i), src)
                    vcnt.append(k.c["V"])
                k.on("vector", vb3)

        def final_out():
            def yout(y):
                for i in range(TT):
                    k.inc(y.dma_start(out=out[i * 128:(i + 1) * 128, :], in_=R[:, i * 1024:(i + 1) * 1024]), "dY")
                y.wait("dY")
            k.on("sync", yout)

        def pipe(units, nbanks):
            tcnt, vcnt = [], []
            for u, (emm, eev, eeng) in enumerate(units):
                def tb(t, u=u, emm=emm):
                    if u >= nbanks:
                        t.wait(SEMN[units[u - nbanks][2]], vcnt[u - nbanks])
                    emm(t)
                    tcnt.append(k.c["T"])
                k.on("tensor", tb)

                def vb(v, u=u, eev=eev):
                    v.wait("T", tcnt[u])
                    eev(v)
                    vcnt.append(k.c[SEMN[eeng]])
                k.on(eeng, vb)

        def sub_l0aa():
            with nc.sbuf_tensor("wb", [128, 3 * 8 * 1024], BF16) as wb_t, \
                 nc.sbuf_tensor("brow", [1, 4 * 1024], BF16) as brow_t, \
                 nc.sbuf_tensor("Msb", [128, 512], BF16) as Msb_t:
                wb, brow, Msb = wb_t[:, :], brow_t[:, :], Msb_t[:, :]

                def wsl(widx, c, nb):
                    return wb[:, widx * 8192 + c * 1024 + nb * 512: widx * 8192 + c * 1024 + nb * 512 + 512]

                with nc.sbuf_tensor("stage", [128, 2 * 2048], F32) as stage_t:
                    stage = stage_t[:, :]
                    load_cast(stage, [(I[n], (lambda i0: lambda c: wb[:, i0 * 8192 + c * 1024:i0 * 8192 + (c + 1) * 1024])(i0), D, 1024)
                                      for i0, n in enumerate(("aaWq", "aaWk", "aaWv"))])
                    load_rows_cast(stage, [(I["aab"][i:i + 1, :], brow[0:1, i * 1024:(i + 1) * 1024], 1024) for i in range(3)])
                    with nc.psum_tensor("psb", [128, 2048], BF16) as psb:
                        emit_ln(psb[:, :])
                    k.barrier()

                with nc.sbuf_tensor("qkvt", [128, 2 * 3072], BF16) as qkvt_t, \
                     nc.sbuf_tensor("krt", [128, 2 * 1024], BF16) as krt_t, \
                     nc.sbuf_tensor("qrt", [128, 2 * 1024], BF16) as qrt_t, \
                     nc.sbuf_tensor("ropeb", [128, 2 * 1024], F32) as ropeb_t, \
                     nc.sbuf_tensor("ttmp", [128, 2048], F32) as ttmp_t, \
                     nc.psum_tensor("pp", [128, 4 * 512], F32) as pp_t, \
                     nc.psum_tensor("pM", [128, 1024], F32) as pM_t, \
                     nc.psum_tensor("ptp", [128, 2048], BF16) as ptp_t:
                    qkvt, krt, qrt = qkvt_t[:, :], krt_t[:, :], qrt_t[:, :]
                    ropeb, ttmp, pp, pM, ptp = ropeb_t[:, :], ttmp_t[:, :], pp_t[:, :], pM_t[:, :], ptp_t[:, :]

                    pcnt, ecnt, rcnt, mcnt, qtcnt, dcnt = [], [], [], [], [], []
                    for i in range(TT):
                        def yb(y, i=i):
                            if i >= 2:
                                y.wait("V", rcnt[i - 2])
                            k.inc(y.dma_start(out=ropeb[:, (i % 2) * 1024:(i % 2) * 1024 + 512],
                                              in_=I["ropeAc"][i * 128:(i + 1) * 128, :]), "dY%d" % (i % 2))
                            k.inc(y.dma_start(out=ropeb[:, (i % 2) * 1024 + 512:(i % 2) * 1024 + 1024],
                                              in_=I["ropeAs"][i * 128:(i + 1) * 128, :]), "dY%d" % (i % 2))
                            dcnt.append(k.c["dY%d" % (i % 2)])
                        k.on("sync", yb)

                        for u6 in range(6):
                            j, nb = u6 // 2, u6 % 2
                            def tb(t, i=i, u6=u6, j=j, nb=nb):
                                u = 6 * i + u6
                                if u >= 4:
                                    t.wait("V", ecnt[u - 4])
                                bank = pp[:, (u % 4) * 512:(u % 4) * 512 + 512]
                                for c in range(FC):
                                    t.matmul(bank, xlnT[:, c * 2048 + i * 128: c * 2048 + i * 128 + 128],
                                             wsl(j, c, nb), start=(c == 0), stop=False)
                                t.matmul(bank, onesb[0:1, 0:128],
                                         brow[0:1, j * 1024 + nb * 512: j * 1024 + nb * 512 + 512],
                                         start=False, stop=True)
                                pcnt.append(k.c["T"])
                            k.on("tensor", tb)

                            def vb(v, i=i, u6=u6, j=j, nb=nb):
                                u = 6 * i + u6
                                v.wait("T", pcnt[u])
                                if u6 == 0 and i >= 2:
                                    v.wait("T", mcnt[i - 2])
                                bank = pp[:, (u % 4) * 512:(u % 4) * 512 + 512]
                                dst = qkvt[:, (i % 2) * 3072 + j * 1024 + nb * 512:(i % 2) * 3072 + j * 1024 + nb * 512 + 512]
                                v.tensor_copy(dst, bank)
                                ecnt.append(k.c["V"])
                            k.on("vector", vb)

                        def vrope(v, i=i):
                            v.wait("dY%d" % (i % 2), dcnt[i])
                            v.wait("V")
                            cc = ropeb[:, (i % 2) * 1024:(i % 2) * 1024 + 512]
                            ss = ropeb[:, (i % 2) * 1024 + 512:(i % 2) * 1024 + 1024]
                            for (soff, dst) in ((0, qrt[:, (i % 2) * 1024:(i % 2) * 1024 + 1024]),
                                                (1024, krt[:, (i % 2) * 1024:(i % 2) * 1024 + 1024])):
                                src = qkvt[:, (i % 2) * 3072 + soff:(i % 2) * 3072 + soff + 1024]
                                se, so = src[:, 0:1024:2], src[:, 1:1024:2]
                                de, do = dst[:, 0:1024:2], dst[:, 1:1024:2]
                                v.wait("V")
                                v.tensor_tensor(ttmp[:, 0:512], se, cc, OP.mult)
                                v.tensor_tensor(ttmp[:, 512:1024], so, ss, OP.mult)
                                v.tensor_tensor(ttmp[:, 1024:1536], se, ss, OP.mult)
                                v.tensor_tensor(ttmp[:, 1536:2048], so, cc, OP.mult)
                                v.wait("V")
                                v.tensor_tensor(de, ttmp[:, 0:512], ttmp[:, 512:1024], OP.subtract)
                                v.tensor_tensor(do, ttmp[:, 1024:1536], ttmp[:, 1536:2048], OP.add)
                            rcnt.append(k.c["V"])
                        k.on("vector", vrope)

                        def tM(t, i=i):
                            t.wait("V", rcnt[i])
                            for h in range(16):
                                po = 64 * (h % 2)
                                t.matmul(pM[po:po + 64, po * 8 + (h // 2) * 64: po * 8 + (h // 2) * 64 + 64],
                                         krt[:, (i % 2) * 1024 + h * 64:(i % 2) * 1024 + h * 64 + 64],
                                         qkvt[:, (i % 2) * 3072 + 2048 + h * 64:(i % 2) * 3072 + 2048 + h * 64 + 64],
                                         start=(i == 0 and h in (0, 1)), stop=(i == TT - 1 and h in (14, 15)),
                                         tile_position=(0, po))
                            if i >= 2:
                                t.wait("V", qtcnt[i - 2])
                            for c in range(FC):
                                t.transpose(ptp[:, (i % 2) * 1024 + c * 128:(i % 2) * 1024 + c * 128 + 128],
                                            qrt[:, (i % 2) * 1024 + c * 128:(i % 2) * 1024 + c * 128 + 128],
                                            identb[:, :])
                            mcnt.append(k.c["T"])
                        k.on("tensor", tM)

                        def vtp(v, i=i):
                            v.wait("T", mcnt[i])
                            src = ptp[:, (i % 2) * 1024:(i % 2) * 1024 + 1024].rearrange("p (c s) -> p c s", c=FC)
                            v.tensor_copy(xlnT_grid(i), src)
                            qtcnt.append(k.c["V"])
                        k.on("vector", vtp)
                    k.barrier()

                    def vM(v):
                        v.tensor_scalar(Msb[0:64, :], pM[0:64, 0:512], 0.125, None, OP.mult)
                        v.tensor_scalar(Msb[64:128, :], pM[64:128, 512:1024], 0.125, None, OP.mult)
                    k.on("vector", vM)
                    k.barrier()

                with nc.sbuf_tensor("stage2", [128, 2 * 2048], F32) as stage2_t:
                    stage2 = stage2_t[:, :]
                    load_cast(stage2, [(I["aaWo"], lambda c: wb[:, c * 1024:(c + 1) * 1024], D, 1024)])
                    load_rows_cast(stage2, [(I["aab"][3:4, :], brow[0:1, 3 * 1024:4 * 1024], 1024)])
                    k.barrier()

                qrT = xlnT
                with nc.sbuf_tensor("ctxT", [128, FC * 2048], BF16) as ctxT_t, \
                     nc.psum_tensor("pc", [128, 4 * 512], F32) as pc_t:
                    ctxT, pc = ctxT_t[:, :], pc_t[:, :]
                    units = []
                    for u in range(64):
                        h, tb_ = u // 4, u % 4
                        j, po = h // 2, 64 * (h % 2)

                        def emm(t, u=u, j=j, po=po, tb_=tb_):
                            bank = pc[:, (u % 4) * 512:(u % 4) * 512 + 512]
                            t.matmul(bank[po:po + 64, :], Msb[po:po + 64, j * 64:(j + 1) * 64],
                                     qrT[po:po + 64, j * 2048 + tb_ * 512: j * 2048 + tb_ * 512 + 512],
                                     start=True, stop=True, tile_position=(po, po))

                        def eev(v, u=u, j=j, po=po, tb_=tb_):
                            bank = pc[:, (u % 4) * 512:(u % 4) * 512 + 512]
                            v.tensor_copy(ctxT[po:po + 64, j * 2048 + tb_ * 512: j * 2048 + tb_ * 512 + 512],
                                          bank[po:po + 64, :])
                        units.append((emm, eev, "vector"))
                    pipe(units, 4)
                    k.barrier()

                    with nc.psum_tensor("po", [128, 4 * 512], F32) as po_t:
                        po = po_t[:, :]
                        units = []
                        for u in range(32):
                            i, nb = u // 2, u % 2

                            def emm(t, u=u, i=i, nb=nb):
                                bank = po[:, (u % 4) * 512:(u % 4) * 512 + 512]
                                for c in range(FC):
                                    t.matmul(bank, ctxT[:, c * 2048 + i * 128: c * 2048 + i * 128 + 128],
                                             wsl(0, c, nb), start=(c == 0), stop=False)
                                t.matmul(bank, onesb[0:1, 0:128],
                                         brow[0:1, 3 * 1024 + nb * 512: 3 * 1024 + nb * 512 + 512],
                                         start=False, stop=True)

                            def eev(v, u=u, i=i, nb=nb):
                                bank = po[:, (u % 4) * 512:(u % 4) * 512 + 512]
                                rsl = R[:, i * 1024 + nb * 512: i * 1024 + nb * 512 + 512]
                                v.scalar_tensor_tensor(rsl, bank, 0.0, rsl, OP.add, OP.add)
                            units.append((emm, eev, "vector"))
                        pipe(units, 4)
                    k.barrier()
        def park_R():
            def yp(y):
                for i in range(TT):
                    k.inc(y.dma_start(out=d_park[i * 128:(i + 1) * 128, :],
                                      in_=R[:, i * 1024:(i + 1) * 1024]), "dY")
            k.on("sync", yp)
            k.barrier()
            free_R()

        def unpark_R():
            nonlocal R
            alloc_R()
            R = RH["ap"]

            def yu(y):
                for i in range(TT):
                    k.inc(y.dma_start(out=R[:, i * 1024:(i + 1) * 1024],
                                      in_=d_park[i * 128:(i + 1) * 128, :]), "dY")
            k.on("sync", yu)
            k.barrier()

        def emit_rope(v, src, dst, cc, ss, ttmp):
            se, so = src[:, 0:1024:2], src[:, 1:1024:2]
            de, do = dst[:, 0:1024:2], dst[:, 1:1024:2]
            v.wait("V")
            v.tensor_tensor(ttmp[:, 0:512], se, cc, OP.mult)
            v.tensor_tensor(ttmp[:, 512:1024], so, ss, OP.mult)
            v.tensor_tensor(ttmp[:, 1024:1536], se, ss, OP.mult)
            v.tensor_tensor(ttmp[:, 1536:2048], so, cc, OP.mult)
            v.wait("V")
            v.tensor_tensor(de, ttmp[:, 0:512], ttmp[:, 512:1024], OP.subtract)
            v.tensor_tensor(do, ttmp[:, 1024:1536], ttmp[:, 1536:2048], OP.add)

        def sub_l0ffn():
            with nc.sbuf_tensor("kb", [128, FC * 1024], BF16) as kb_t, \
                 nc.sbuf_tensor("kq8", [128, FC * 1024], FP8) as kq8_t, \
                 nc.sbuf_tensor("xq8", [128, FC * 2048], FP8) as xq8_t, \
                 nc.sbuf_tensor("bbc", [128, 1024], BF16) as bbc_t, \
                 nc.sbuf_tensor("brf", [1, 1024], BF16) as brf_t, \
                 nc.sbuf_tensor("sm2", [128, 4], F32) as sm2_t, \
                 nc.sbuf_tensor("tb2", [128, 2 * 512], F32) as tb2_t:
                kb, kq8, xq8, bbc = kb_t[:, :], kq8_t[:, :], xq8_t[:, :], bbc_t[:, :]
                brf, sm2, tb2 = brf_t[:, :], sm2_t[:, :], tb2_t[:, :]

                def yk(y):
                    for c in range(FC):
                        k.inc(y.dma_start(out=kb[:, c * 1024:(c + 1) * 1024],
                                          in_=I["f0k"][c * 128:(c + 1) * 128, :]), "dY")
                    k.inc(y.dma_start(out=sm2[0:1, 0:2], in_=I["f0scales"][0:1, :]), "dY")
                    k.inc(y.dma_start(out=brf[0:1, 0:1024], in_=I["f0bias"][0:1, :]), "dY")
                k.on("sync", yk)
                k.barrier()

                def vsc(v):
                    v.reciprocal(sm2[0:1, 2:4], sm2[0:1, 0:2])
                    v.wait("V")
                    v.tensor_tensor(sm2[0:1, 0:1], sm2[0:1, 0:1], sm2[0:1, 1:2], OP.mult)
                k.on("vector", vsc)

                with nc.psum_tensor("pbc", [128, 512], F32) as pbc_t:
                    pbc = pbc_t[:, :]

                    def tsc(t):
                        t.wait("V")
                        t.matmul(pbc[:, 0:4], onesf[0:1, :], sm2[0:1, 0:4], start=True, stop=True)
                    k.on("tensor", tsc)

                    def vsc2(v):
                        v.wait("T")
                        v.tensor_copy(sm2[:, 0:4], pbc[:, 0:4])
                    k.on("vector", vsc2)
                    k.barrier()
                k.barrier()

                def vq(s):
                    for c in range(FC):
                        s.wait("S")
                        s.activation(out=kq8[:, c * 1024:(c + 1) * 1024], in_=kb[:, c * 1024:(c + 1) * 1024],
                                     func=AF.Identity, scale=sm2[:, 3:4])
                k.on("scalar", vq)
                with nc.psum_tensor("psb", [128, 2048], BF16) as psb:
                    emit_ln(psb[:, :])
                k.barrier()

                def vx(s):
                    for c in range(FC):
                        s.wait("S")
                        s.activation(out=xq8[:, c * 2048:(c + 1) * 2048], in_=xlnT[:, c * 2048:(c + 1) * 2048],
                                     func=AF.Identity, scale=sm2[:, 2:3])
                k.on("scalar", vx)

                with nc.psum_tensor("pbc2", [128, 2 * 512], F32) as pbc2_t:
                    pbc2 = pbc2_t[:, :]

                    def tbc2(t):
                        for b2 in range(2):
                            t.matmul(pbc2[:, b2 * 512:(b2 + 1) * 512], onesb[0:1, 0:128],
                                     brf[0:1, b2 * 512:(b2 + 1) * 512], start=True, stop=True)
                    k.on("tensor", tbc2)

                    def vb2c(v):
                        v.wait("T")
                        v.tensor_copy(bbc[:, 0:1024], pbc2[:, 0:1024])
                    k.on("vector", vb2c)
                    k.barrier()
                k.barrier()

                with nc.psum_tensor("py", [128, 4 * 512], F32) as py_t:
                    py = py_t[:, :]
                    units = []
                    for u in range(32):
                        i, nb = u // 2, u % 2

                        def emm(t, u=u, i=i, nb=nb):
                            bank = py[:, (u % 4) * 512:(u % 4) * 512 + 512]
                            for c in range(FC):
                                t.matmul(bank, xq8[:, c * 2048 + i * 128: c * 2048 + i * 128 + 128],
                                         kq8[:, c * 1024 + nb * 512: c * 1024 + nb * 512 + 512],
                                         start=(c == 0), stop=(c == FC - 1))

                        def eev(v, u=u, i=i, nb=nb):
                            bank = py[:, (u % 4) * 512:(u % 4) * 512 + 512]
                            rsl = R[:, i * 1024 + nb * 512: i * 1024 + nb * 512 + 512]
                            tb = tb2[:, (u % 2) * 512:(u % 2) * 512 + 512]
                            v.wait("V")
                            v.scalar_tensor_tensor(tb, bank, sm2[:, 0:1], bbc[:, nb * 512:nb * 512 + 512],
                                                   OP.mult, OP.add)
                            v.wait("V")
                            v.tensor_tensor(rsl, rsl, tb, OP.add)
                        units.append((emm, eev, "vector"))
                    pipe(units, 4)
                k.barrier()
        def sub_l0ab():
            with nc.sbuf_tensor("yab", [128, TT * 1024], BF16) as y_t:
                yab = y_t[:, :]
                with nc.psum_tensor("psb", [128, 2048], BF16) as psb:
                    emit_ln(psb[:, :])
                k.barrier()
                park_R()
                with nc.sbuf_tensor("Kfm", [128, FC * 2048], BF16) as Kfm_t, \
                     nc.sbuf_tensor("Vaug", [128, 16 * 1040], BF16) as Vaug_t, \
                     nc.sbuf_tensor("wb2", [128, 2 * 8192], BF16) as wb2_t, \
                     nc.sbuf_tensor("brow2", [1, 3072], BF16) as brow2_t:
                    Kfm, Vaug, wb2, brow2 = Kfm_t[:, :], Vaug_t[:, :], wb2_t[:, :], brow2_t[:, :]

                    def kfm_grid(i):
                        return Kfm_t.ap().rearrange("p (c s) -> p c s", c=FC)[:, :, i * 128:(i + 1) * 128]

                    with nc.sbuf_tensor("stg", [128, 2 * 2048], F32) as stg_t:
                        stg = stg_t[:, :]
                        load_cast(stg, [(I["abWv"], lambda c: wb2[:, c * 1024:(c + 1) * 1024], D, 1024)])
                        load_rows_cast(stg, [(I["abb"][2:3, :], brow2[0:1, 2048:3072], 1024)])
                        k.barrier()

                        with nc.psum_tensor("pv", [128, 4 * 512], F32) as pv_t:
                            pv = pv_t[:, :]
                            units = []
                            for u in range(32):
                                i, nb = u // 2, u % 2

                                def emm(t, u=u, i=i, nb=nb):
                                    bank = pv[:, (u % 4) * 512:(u % 4) * 512 + 512]
                                    for c in range(FC):
                                        t.matmul(bank, xlnT[:, c * 2048 + i * 128: c * 2048 + i * 128 + 128],
                                                 wb2[:, c * 1024 + nb * 512: c * 1024 + nb * 512 + 512],
                                                 start=(c == 0), stop=False)
                                    t.matmul(bank, onesb[0:1, 0:128], brow2[0:1, 2048 + nb * 512: 2048 + nb * 512 + 512],
                                             start=False, stop=True)

                                def eev(v, u=u, i=i, nb=nb):
                                    bank = pv[:, (u % 4) * 512:(u % 4) * 512 + 512]
                                    dst = Vaug_t.ap().rearrange("p (a b) -> p a b", a=16)[
                                        :, i:i + 1, nb * 8 * 65: nb * 8 * 65 + 520].rearrange(
                                        "p a (h e) -> p a h e", h=8)[:, 0, :, 0:64]
                                    v.tensor_copy(dst, bank.rearrange("p (h e) -> p h e", h=8))
                                units.append((emm, eev, "vector"))
                            pipe(units, 4)
                        def vones(v):
                            v.wait("V")
                            ap = Vaug_t.ap().rearrange("p (a b) -> p a b", a=16)[:, :, 64::65]
                            v.memset(ap, 1.0)
                        k.on("vector", vones)
                        load_cast(stg, [(I["abWq"], lambda c: wb2[:, c * 1024:(c + 1) * 1024], D, 1024),
                                        (I["abWk"], lambda c: wb2[:, 8192 + c * 1024:8192 + (c + 1) * 1024], D, 1024)],
                                  after=("T", k.c["T"]))
                        load_rows_cast(stg, [(I["abb"][0:1, :], brow2[0:1, 0:1024], 1024),
                                             (I["abb"][1:2, :], brow2[0:1, 1024:2048], 1024)])
                        k.barrier()

                    with nc.sbuf_tensor("qkt", [128, 2 * 2048], BF16) as qkt_t, \
                         nc.sbuf_tensor("rqt", [128, 2 * 2048], BF16) as rqt_t, \
                         nc.sbuf_tensor("ropeb", [128, 2 * 1024], F32) as ropeb_t, \
                         nc.sbuf_tensor("ttmp2", [128, 2048], BF16) as ttmp2_t, \
                         nc.psum_tensor("pq", [128, 4 * 512], F32) as pq_t, \
                         nc.psum_tensor("ptp2", [128, 2048], BF16) as ptp2_t:
                        qkt, rqt, ropeb, ttmp2 = qkt_t[:, :], rqt_t[:, :], ropeb_t[:, :], ttmp2_t[:, :]
                        pq, ptp2 = pq_t[:, :], ptp2_t[:, :]
                        pcnt, ecnt, rcnt, tcnt, vcnt, dcnt = [], [], [], [], [], []
                        for i in range(TT):
                            def yb(y, i=i):
                                if i >= 2:
                                    y.wait("V", rcnt[i - 2])
                                k.inc(y.dma_start(out=ropeb[:, (i % 2) * 1024:(i % 2) * 1024 + 512],
                                                  in_=I["ropeBc"][i * 128:(i + 1) * 128, :]), "dY%d" % (i % 2))
                                k.inc(y.dma_start(out=ropeb[:, (i % 2) * 1024 + 512:(i % 2) * 1024 + 1024],
                                                  in_=I["ropeBs"][i * 128:(i + 1) * 128, :]), "dY%d" % (i % 2))
                                dcnt.append(k.c["dY%d" % (i % 2)])
                            k.on("sync", yb)

                            for u4 in range(4):
                                j, nb = u4 // 2, u4 % 2

                                def tb(t, i=i, u4=u4, j=j, nb=nb):
                                    u = 4 * i + u4
                                    if u >= 4:
                                        t.wait("V", ecnt[u - 4])
                                    bank = pq[:, (u % 4) * 512:(u % 4) * 512 + 512]
                                    for c in range(FC):
                                        t.matmul(bank, xlnT[:, c * 2048 + i * 128: c * 2048 + i * 128 + 128],
                                                 wb2[:, j * 8192 + c * 1024 + nb * 512: j * 8192 + c * 1024 + nb * 512 + 512],
                                                 start=(c == 0), stop=False)
                                    t.matmul(bank, onesb[0:1, 0:128],
                                             brow2[0:1, j * 1024 + nb * 512: j * 1024 + nb * 512 + 512],
                                             start=False, stop=True)
                                    pcnt.append(k.c["T"])
                                k.on("tensor", tb)

                                def vb(v, i=i, u4=u4, j=j, nb=nb):
                                    u = 4 * i + u4
                                    v.wait("T", pcnt[u])
                                    if u4 == 0 and i >= 2:
                                        v.wait("T", tcnt[i - 2])
                                    if u4 == 0:
                                        v.wait("V")
                                    bank = pq[:, (u % 4) * 512:(u % 4) * 512 + 512]
                                    dst = qkt[:, (i % 2) * 2048 + j * 1024 + nb * 512:(i % 2) * 2048 + j * 1024 + nb * 512 + 512]
                                    v.tensor_copy(dst, bank)
                                    ecnt.append(k.c["V"])
                                k.on("vector", vb)

                            def vrope(v, i=i):
                                v.wait("dY%d" % (i % 2), dcnt[i])
                                cc = ropeb[:, (i % 2) * 1024:(i % 2) * 1024 + 512]
                                ss = ropeb[:, (i % 2) * 1024 + 512:(i % 2) * 1024 + 1024]
                                emit_rope(v, qkt[:, (i % 2) * 2048:(i % 2) * 2048 + 1024],
                                          rqt[:, (i % 2) * 2048:(i % 2) * 2048 + 1024], cc, ss, ttmp2)
                                emit_rope(v, qkt[:, (i % 2) * 2048 + 1024:(i % 2) * 2048 + 2048],
                                          rqt[:, (i % 2) * 2048 + 1024:(i % 2) * 2048 + 2048], cc, ss, ttmp2)
                                rcnt.append(k.c["V"])
                            k.on("vector", vrope)

                            def ttp(t, i=i):
                                t.wait("V", rcnt[i])
                                if i >= 1:
                                    t.wait("V", vcnt[i - 1])
                                for c in range(FC):
                                    t.transpose(ptp2[:, c * 128:c * 128 + 128],
                                                rqt[:, (i % 2) * 2048 + c * 128:(i % 2) * 2048 + c * 128 + 128],
                                                identb[:, :])
                                for c in range(FC):
                                    t.transpose(ptp2[:, 1024 + c * 128:1024 + c * 128 + 128],
                                                rqt[:, (i % 2) * 2048 + 1024 + c * 128:(i % 2) * 2048 + 1024 + c * 128 + 128],
                                                identb[:, :])
                                tcnt.append(k.c["T"])
                            k.on("tensor", ttp)

                            def vtp(v, i=i):
                                v.wait("T", tcnt[i])
                                v.tensor_copy(xlnT_grid(i), ptp2[:, 0:1024].rearrange("p (c s) -> p c s", c=FC))
                                v.tensor_copy(kfm_grid(i), ptp2[:, 1024:2048].rearrange("p (c s) -> p c s", c=FC))
                                vcnt.append(k.c["V"])
                            k.on("vector", vtp)
                        k.barrier()

                    Qfm = xlnT
                    with nc.sbuf_tensor("stg2", [128, 2 * 2048], F32) as stg2_t:
                        load_cast(stg2_t[:, :],
                                  [(I["abWoPk"][:, cb * 2048:(cb + 1) * 2048],
                                    (lambda cb: lambda c: wb2[0:64, cb * 2048:(cb + 1) * 2048])(cb), 64, 2048)
                                   for cb in range(8)])
                        load_rows_cast(stg2_t[:, :], [(I["abb"][3:4, :], brow2[0:1, 0:1024], 1024)])
                        k.barrier()

                    with nc.sbuf_tensor("Pt", [128, 2 * 2048], BF16) as Pt_t, \
                         nc.sbuf_tensor("cst", [128, 16 * 512], BF16) as cst_t, \
                         nc.sbuf_tensor("lsc", [128, 512], F32) as lsc_t, \
                         nc.sbuf_tensor("lb", [128, 512], F32) as lb_t, \
                         nc.psum_tensor("pS", [128, 2048], F32) as pS_t, \
                         nc.psum_tensor("pC", [128, 2 * 512], F32) as pC_t, \
                         nc.psum_tensor("pO", [128, 2 * 512], F32) as pO_t:
                        Pt, cst, lsc, lb = Pt_t[:, :], cst_t[:, :], lsc_t[:, :], lb_t[:, :]
                        pS, pC, pO = pS_t[:, :], pC_t[:, :], pO_t[:, :]
                        scnt2, pcnt2, vcnt2, ccnt, gcnt2, dmac = {}, {}, {}, {}, {}, {}
                        ocnt, expseq, pvseq = [], [], []
                        for qb in range(4):
                            for h in range(16):
                                po = 64 * (h % 2)
                                j = h // 2
                                for kg in range(4):
                                    gidx = (qb * 16 + h) * 4 + kg

                                    def tS(t, qb=qb, h=h, po=po, j=j, kg=kg, gidx=gidx):
                                        if gidx >= 1:
                                            t.wait("S", expseq[gidx - 1])
                                        for kc in range(4 * kg, 4 * kg + 4):
                                            t.matmul(pS[:, (kc % 4) * 512:(kc % 4) * 512 + 512],
                                                     Kfm[po:po + 64, j * 2048 + kc * 128: j * 2048 + kc * 128 + 128],
                                                     Qfm[po:po + 64, j * 2048 + qb * 512: j * 2048 + qb * 512 + 512],
                                                     start=True, stop=True, tile_position=(po, 0))
                                        pcnt2[(qb, h, kg)] = k.c["T"]
                                    k.on("tensor", tS)

                                    def sE(s, qb=qb, h=h, kg=kg, gidx=gidx):
                                        s.wait("T", pcnt2[(qb, h, kg)])
                                        if gidx >= 2:
                                            s.wait("T", pvseq[gidx - 2])
                                        s.activation(out=Pt[:, (gidx % 2) * 2048:(gidx % 2) * 2048 + 2048],
                                                     in_=pS[:, :], func=AF.Exp, scale=0.125)
                                        scnt2[(qb, h, kg)] = k.c["S"]
                                        expseq.append(k.c["S"])
                                    k.on("scalar", sE)

                                    def tPV(t, qb=qb, h=h, kg=kg, gidx=gidx):
                                        t.wait("S", scnt2[(qb, h, kg)])
                                        if kg == 0 and qb * 16 + h >= 2:
                                            hh = qb * 16 + h - 2
                                            t.wait("V", ccnt[(hh // 16, hh % 16)])
                                        for kc in range(4 * kg, 4 * kg + 4):
                                            t.matmul(pC[0:65, (h % 2) * 512:(h % 2) * 512 + 512],
                                                     Vaug[:, kc * 1040 + h * 65: kc * 1040 + h * 65 + 65],
                                                     Pt[:, (gidx % 2) * 2048 + (kc % 4) * 512:(gidx % 2) * 2048 + (kc % 4) * 512 + 512],
                                                     start=(kg == 0 and kc == 0), stop=(kg == 3 and kc == 4 * kg + 3))
                                        pcnt2[(qb, h, "pv", kg)] = k.c["T"]
                                        pvseq.append(k.c["T"])
                                    k.on("tensor", tPV)

                                def vL(v, qb=qb, h=h):
                                    v.wait("T", pcnt2[(qb, h, "pv", 3)])
                                    hh = qb * 16 + h - 1
                                    if hh >= 0 and (hh // 16, hh % 16) in dmac:
                                        v.wait("dY%d" % ((hh % 16) % 2), dmac[(hh // 16, hh % 16)])
                                    v.wait("V")
                                    v.tensor_copy(lsc[64:65, 0:512], pC[64:65, (h % 2) * 512:(h % 2) * 512 + 512])
                                    vcnt2[(qb, h, "l")] = k.c["V"]
                                k.on("vector", vL)

                                def yL(y, qb=qb, h=h):
                                    y.wait("V", vcnt2[(qb, h, "l")])
                                    k.inc(y.dma_start(out=lsc[0:1, 0:512], in_=lsc[64:65, 0:512]), "dY%d" % (h % 2))
                                    dmac[(qb, h)] = k.c["dY%d" % (h % 2)]
                                k.on("sync", yL)

                                def vR(v, qb=qb, h=h):
                                    v.wait("dY%d" % (h % 2), dmac[(qb, h)])
                                    v.reciprocal(lsc[0:1, 0:512], lsc[0:1, 0:512])
                                    vcnt2[(qb, h, "r")] = k.c["V"]
                                k.on("vector", vR)

                                def gB(t, qb=qb, h=h):
                                    t.wait("V", vcnt2[(qb, h, "r")])
                                    if qb * 16 + h >= 1:
                                        t.wait("V")
                                    t.matmul(pO[0:64, 0:512], onesf[0:1, 0:64], lsc[0:1, 0:512],
                                             start=True, stop=True)
                                    gcnt2[(qb, h)] = k.c["T"]
                                k.on("tensor", gB)

                                def vB(v, qb=qb, h=h):
                                    v.wait("T", gcnt2[(qb, h)])
                                    v.tensor_copy(lb[0:64, 0:512], pO[0:64, 0:512])
                                    gcnt2[(qb, h, "c")] = k.c["V"]
                                k.on("vector", vB)

                                def vC(v, qb=qb, h=h):
                                    v.wait("V", gcnt2[(qb, h, "c")])
                                    if h == 0 and qb > 0:
                                        v.wait("T", pcnt2[(qb - 1, "o", 7)])
                                    v.tensor_tensor(cst[0:64, h * 512:(h + 1) * 512],
                                                    pC[0:64, (h % 2) * 512:(h % 2) * 512 + 512],
                                                    lb[0:64, 0:512], OP.mult)
                                    ccnt[(qb, h)] = k.c["V"]
                                k.on("vector", vC)

                            for uo in range(8):
                                sc, nb = uo // 2, uo % 2

                                def tO(t, qb=qb, uo=uo, sc=sc, nb=nb):
                                    t.wait("V", ccnt[(qb, 15)])
                                    g = qb * 8 + uo
                                    if g >= 2:
                                        t.wait("V", ocnt[g - 2])
                                    bank = pO[:, (uo % 2) * 512:(uo % 2) * 512 + 512]
                                    for h in range(16):
                                        t.matmul(bank, cst[0:64, h * 512 + sc * 128: h * 512 + sc * 128 + 128],
                                                 wb2[0:64, h * 1024 + nb * 512: h * 1024 + nb * 512 + 512],
                                                 start=(h == 0), stop=False)
                                    t.matmul(bank, onesb[0:1, 0:128], brow2[0:1, nb * 512: nb * 512 + 512],
                                             start=False, stop=True)
                                    pcnt2[(qb, "o", uo)] = k.c["T"]
                                k.on("tensor", tO)

                                def vO(v, qb=qb, uo=uo, sc=sc, nb=nb):
                                    v.wait("T", pcnt2[(qb, "o", uo)])
                                    bank = pO[:, (uo % 2) * 512:(uo % 2) * 512 + 512]
                                    v.tensor_copy(yab[:, (qb * 4 + sc) * 1024 + nb * 512:(qb * 4 + sc) * 1024 + nb * 512 + 512], bank)
                                    ocnt.append(k.c["V"])
                                k.on("vector", vO)
                        k.barrier()
                unpark_R()

                def vres(v):
                    for i in range(TT):
                        v.tensor_tensor(R[:, i * 1024:(i + 1) * 1024], R[:, i * 1024:(i + 1) * 1024],
                                        yab[:, i * 1024:(i + 1) * 1024], OP.add)
                k.on("vector", vres)
                k.barrier()

        l1f_state = {"wq_done": False}

        def quant_weights():
            with nc.sbuf_tensor("qst", [128, 2 * 4096], F32) as qst_t, \
                 nc.sbuf_tensor("qsc", [128, 16], F32) as qsc_t, \
                 nc.sbuf_tensor("qbf", [128, 2 * 4096], BF16) as qbf_t, \
                 nc.psum_tensor("pq1", [128, 512], F32) as pq1_t:
                qst, qsc, qbf, pq1 = qst_t[:, :], qsc_t[:, :], qbf_t[:, :], pq1_t[:, :]
                for (srcT, dst, rows, cols, nelem, sci) in (
                        (I["w1T"], d_w1q, D, 4096, 4096.0 * 1024.0, 0),
                        (I["w2T"], d_w2q, 4096, D, 4096.0 * 1024.0, 1)):
                    nch = rows // 128
                    dcnt, scnt = [], []
                    def vz(v, sci=sci):
                        v.wait("V")
                        v.memset(qsc[:, 4 * sci:4 * sci + 1], 0.0)
                    k.on("vector", vz)
                    for c in range(nch):
                        def yb(y, c=c, srcT=srcT, cols=cols):
                            if c >= 2:
                                y.wait("S", scnt[c - 2])
                            k.inc(y.dma_start(out=qst[:, (c % 2) * 4096:(c % 2) * 4096 + cols],
                                              in_=srcT[c * 128:(c + 1) * 128, :]), "dY%d" % (c % 2))
                            dcnt.append(k.c["dY%d" % (c % 2)])
                        k.on("sync", yb)

                        def sb(s, c=c, cols=cols, sci=sci):
                            s.wait("dY%d" % (c % 2), dcnt[c])
                            s.wait("S")
                            s.wait("V")
                            s.activation(out=qbf[:, (c % 2) * 4096:(c % 2) * 4096 + cols],
                                         in_=qst[:, (c % 2) * 4096:(c % 2) * 4096 + cols],
                                         func=AF.Abs, accum_out=qsc[:, 4 * sci + 1:4 * sci + 2])
                            scnt.append(k.c["S"])
                        k.on("scalar", sb)

                        def va(v, c=c, sci=sci):
                            v.wait("S", scnt[c])
                            v.wait("V")
                            v.tensor_tensor(qsc[:, 4 * sci:4 * sci + 1], qsc[:, 4 * sci:4 * sci + 1],
                                            qsc[:, 4 * sci + 1:4 * sci + 2], OP.add)
                        k.on("vector", va)

                    def vfin(v, sci=sci):
                        v.wait("V")
                        v.memset(qsc[:, 4 * sci + 2:4 * sci + 3], 1.0)
                    k.on("vector", vfin)

                    def tsum(t, sci=sci):
                        t.wait("V")
                        t.matmul(pq1[0:1, 0:1], qsc[:, 4 * sci:4 * sci + 1], qsc[:, 4 * sci + 2:4 * sci + 3],
                                 start=True, stop=True)
                    k.on("tensor", tsum)

                    def vsc2(v, sci=sci, nelem=nelem):
                        v.wait("T")
                        v.tensor_scalar(small[0:1, 8 + sci:9 + sci], pq1[0:1, 0:1], 1.0 / nelem, 1e-5,
                                        OP.mult, OP.max)
                        v.wait("V")
                        v.reciprocal(qsc[0:1, 4 * sci:4 * sci + 1], small[0:1, 8 + sci:9 + sci])
                    k.on("vector", vsc2)

                    def gb(t, sci=sci):
                        t.wait("V")
                        t.matmul(pq1[:, 4:5], onesf[0:1, :], qsc[0:1, 4 * sci:4 * sci + 1],
                                 start=True, stop=True)
                        t.matmul(pq1[:, 5:6], onesf[0:1, :], small[0:1, 8 + sci:9 + sci],
                                 start=True, stop=True)
                    k.on("tensor", gb)

                    def gbv(v, sci=sci):
                        v.wait("T")
                        v.tensor_copy(qsc[:, 4 * sci:4 * sci + 1], pq1[:, 4:5])
                        v.tensor_copy(small[:, 8 + sci:9 + sci], pq1[:, 5:6])
                    k.on("vector", gbv)
                    k.barrier()
                    dcnt2, vcnt2, ycnt2 = [], [], []
                    for c in range(nch):
                        def yb2(y, c=c, srcT=srcT, cols=cols):
                            if c >= 2:
                                y.wait("V", vcnt2[c - 2])
                            k.inc(y.dma_start(out=qst[:, (c % 2) * 4096:(c % 2) * 4096 + cols],
                                              in_=srcT[c * 128:(c + 1) * 128, :]), "dY%d" % (c % 2))
                            dcnt2.append(k.c["dY%d" % (c % 2)])
                        k.on("sync", yb2)

                        def vq2(v, c=c, cols=cols, sci=sci):
                            v.wait("dY%d" % (c % 2), dcnt2[c])
                            v.wait("V")
                            if c >= 2:
                                v.wait("dA%d" % (c % 2), ycnt2[c - 2])
                            sl = qst[:, (c % 2) * 4096:(c % 2) * 4096 + cols]
                            v.tensor_scalar(sl, sl, qsc[:, 4 * sci:4 * sci + 1], 12582912.0, OP.mult, OP.add)
                            v.wait("V")
                            v.tensor_scalar(sl, sl, 12582912.0, None, OP.subtract)
                            v.wait("V")
                            v.tensor_scalar(qbf[:, (c % 2) * 4096:(c % 2) * 4096 + cols], sl, 1.0, -1.0,
                                            OP.min, OP.max)
                            vcnt2.append(k.c["V"])
                        k.on("vector", vq2)

                        def yb3(y, c=c, dst=dst, cols=cols):
                            y.wait("V", vcnt2[c])
                            k.inc(y.dma_start(out=dst[c * 128:(c + 1) * 128, :],
                                              in_=qbf[:, (c % 2) * 4096:(c % 2) * 4096 + cols]), "dA%d" % (c % 2))
                            ycnt2.append(k.c["dA%d" % (c % 2)])
                        k.on("sync", yb3)
                    k.barrier()

        def sub_l1ffn():
            if not l1f_state["wq_done"]:
                quant_weights()
                l1f_state["wq_done"] = True
            emit_ln_stats()
            k.barrier()
            with nc.sbuf_tensor("st2", [128, 32], F32) as st2_t, \
                 nc.sbuf_tensor("sy1", [128, 16], F32) as sy1_t, \
                 nc.sbuf_tensor("wsb", [1, 16], F32) as wsb_t:
                st2, sy1, wsb = st2_t[:, :], sy1_t[:, :], wsb_t[:, :]
                with nc.sbuf_tensor("xq1T", [128, FC * 2048], BF16) as xq1T_t, \
                     nc.sbuf_tensor("xf", [128, 2 * 1024], F32) as xf_t, \
                     nc.sbuf_tensor("xqk", [128, 2 * 1024], BF16) as xqk_t, \
                     nc.sbuf_tensor("w1s", [128, 2 * 4096], BF16) as w1s_t, \
                     nc.sbuf_tensor("b1b", [128, 4096], F32) as b1b_t, \
                     nc.sbuf_tensor("b1r", [1, 4096], F32) as b1r_t, \
                     nc.sbuf_tensor("hb", [128, 1024], BF16) as hb_t, \
                     nc.sbuf_tensor("tbv", [128, 2 * 512], F32) as tbv_t, \
                     nc.psum_tensor("ptq", [128, 2048], BF16) as ptq_t, \
                     nc.psum_tensor("py1", [128, 4 * 512], F32) as py1_t:
                    xq1T, xf, xqk = xq1T_t[:, :], xf_t[:, :], xqk_t[:, :]
                    w1s, b1b, b1r, hb, tbv = w1s_t[:, :], b1b_t[:, :], b1r_t[:, :], hb_t[:, :], tbv_t[:, :]
                    ptq, py1 = ptq_t[:, :], py1_t[:, :]

                    def xq1T_grid(i):
                        return xq1T_t.ap().rearrange("p (c s) -> p c s", c=FC)[:, :, i * 128:(i + 1) * 128]

                    def yw(y):
                        k.inc(y.dma_start(out=b1r[0:1, :], in_=I["w1b"][0:1, :]), "dY")
                    k.on("sync", yw)

                    def gw(t):
                        t.wait("dY")
                        for b8 in range(4):
                            t.matmul(py1[:, b8 * 512:(b8 + 1) * 512], onesf[0:1, :],
                                     b1r[0:1, b8 * 512:(b8 + 1) * 512], start=True, stop=True)
                    k.on("tensor", gw)
                    def gwv(v):
                        v.wait("T")
                        v.tensor_copy(b1b[:, 0:2048], py1[:, 0:2048])
                    k.on("vector", gwv)
                    k.barrier()

                    def gw2b(t):
                        for b8 in range(4):
                            t.matmul(py1[:, b8 * 512:(b8 + 1) * 512], onesf[0:1, :],
                                     b1r[0:1, 2048 + b8 * 512:2048 + (b8 + 1) * 512], start=True, stop=True)
                    k.on("tensor", gw2b)

                    def gw2v(v):
                        v.wait("T")
                        v.tensor_copy(b1b[:, 2048:4096], py1[:, 0:2048])
                    k.on("vector", gw2v)
                    k.barrier()

                    sacnt, scnt, vcnt, tcnt, ecnt, vqcnt = [], [], [], [], [], []
                    for i in range(TT):
                        def sa(s, i=i):
                            if i >= 2:
                                s.wait("T", tcnt[i - 2])
                            s.wait("V")
                            s.activation(out=xf[:, (i % 2) * 1024:(i % 2) * 1024 + 1024],
                                         in_=R[:, i * 1024:(i + 1) * 1024], func=AF.Identity,
                                         scale=stats[:, 32 + i:33 + i], bias=stats[:, 48 + i:49 + i])
                            s.wait("S")
                            s.activation(out=xqk[:, (i % 2) * 1024:(i % 2) * 1024 + 1024],
                                         in_=xf[:, (i % 2) * 1024:(i % 2) * 1024 + 1024],
                                         func=AF.Square, accum_out=st2[:, 0:1])
                            sacnt.append(k.c["S"])
                        k.on("scalar", sa)

                        def vq(v, i=i):
                            v.wait("S", sacnt[i])
                            v.wait("V")
                            xsl = xf[:, (i % 2) * 1024:(i % 2) * 1024 + 1024]
                            v.reduce_max(st2[:, 1:2], xsl, axis=AX, apply_absolute_value=True)
                            v.tensor_scalar(st2[:, 2:3], st2[:, 0:1], 1.0 / D, 1e-5, OP.mult, OP.add)
                            vqcnt.append(k.c["V"])
                        k.on("vector", vq)

                        def sq(s, i=i):
                            s.wait("V", vqcnt[i])
                            s.activation(out=st2[:, 3:4], in_=st2[:, 2:3], func=AF.Sqrt)
                            scnt.append(k.c["S"])
                        k.on("scalar", sq)

                        def vq2(v, i=i):
                            v.wait("S")
                            v.reciprocal(st2[:, 4:5], st2[:, 3:4])
                            v.wait("V")
                            v.tensor_scalar(st2[:, 5:6], st2[:, 1:2], st2[:, 4:5], 1e-5, OP.mult, OP.max)
                            v.wait("V")
                            v.reciprocal(st2[:, 6:7], st2[:, 5:6])
                            v.tensor_scalar(sy1[:, i:i + 1], st2[:, 5:6], small[:, 8:9], None, OP.mult)
                            v.wait("V")
                            v.tensor_scalar(st2[:, 6:7], st2[:, 6:7], 127.0, None, OP.mult)
                            v.tensor_scalar(sy1[:, i:i + 1], sy1[:, i:i + 1], 1.0 / 127.0, None, OP.mult)
                            v.wait("V")
                            v.tensor_tensor(st2[:, 7:8], st2[:, 4:5], st2[:, 6:7], OP.mult)
                            v.wait("V")
                            xsl = xf[:, (i % 2) * 1024:(i % 2) * 1024 + 1024]
                            v.tensor_scalar(xsl, xsl, st2[:, 7:8], 12582912.0, OP.mult, OP.add)
                            v.wait("V")
                            v.tensor_scalar(xqk[:, (i % 2) * 1024:(i % 2) * 1024 + 1024], xsl,
                                            12582912.0, None, OP.subtract)
                            vcnt.append(k.c["V"])
                        k.on("vector", vq2)

                        def tt(t, i=i):
                            t.wait("V")
                            if i >= 2:
                                t.wait("V", ecnt[i - 2])
                            for c in range(FC):
                                t.transpose(ptq[:, (i % 2) * 1024 + c * 128:(i % 2) * 1024 + c * 128 + 128],
                                            xqk[:, (i % 2) * 1024 + c * 128:(i % 2) * 1024 + c * 128 + 128],
                                            identb[:, :])
                            tcnt.append(k.c["T"])
                        k.on("tensor", tt)

                        def ve(v, i=i):
                            v.wait("T", tcnt[i])
                            v.tensor_copy(xq1T_grid(i), ptq[:, (i % 2) * 1024:(i % 2) * 1024 + 1024].rearrange(
                                "p (c s) -> p c s", c=FC))
                            ecnt.append(k.c["V"])
                        k.on("vector", ve)
                    k.barrier()

                    ycnt3, vcnt3, scnt3, tcnt3, slcnt = [], [], [], [], []
                    sl0 = [k.c["dY0"], k.c["dY1"]]
                    for fb in range(8):
                        def ysl(y, fb=fb):
                            if fb >= 2:
                                y.wait("T", slcnt[fb - 2])
                            for c in range(FC):
                                k.inc(y.dma_start(out=w1s[:, (fb % 2) * 4096 + c * 512:(fb % 2) * 4096 + c * 512 + 512],
                                                  in_=d_w1q[c * 128:(c + 1) * 128, fb * 512:(fb + 1) * 512]),
                                      "dY%d" % (fb % 2))
                        k.on("sync", ysl)
                        for i in range(TT):
                            u = fb * 16 + i

                            def tm(t, u=u, i=i, fb=fb):
                                if i == 0:
                                    t.wait("dY%d" % (fb % 2), sl0[fb % 2] + 16 * FC * (fb // 2 + 1))
                                if u >= 4:
                                    t.wait("V", vcnt3[u - 4])
                                bank = py1[:, (u % 4) * 512:(u % 4) * 512 + 512]
                                for c in range(FC):
                                    t.matmul(bank, xq1T[:, c * 2048 + i * 128: c * 2048 + i * 128 + 128],
                                             w1s[:, (fb % 2) * 4096 + c * 512:(fb % 2) * 4096 + c * 512 + 512],
                                             start=(c == 0), stop=(c == FC - 1))
                                tcnt3.append(k.c["T"])
                                if i == TT - 1:
                                    slcnt.append(k.c["T"])
                            k.on("tensor", tm)

                            def vv(v, u=u, i=i, fb=fb):
                                v.wait("T", tcnt3[u])
                                if u >= 2:
                                    v.wait("S", scnt3[u - 2])
                                bank = py1[:, (u % 4) * 512:(u % 4) * 512 + 512]
                                v.wait("V")
                                v.scalar_tensor_tensor(tbv[:, (u % 2) * 512:(u % 2) * 512 + 512], bank,
                                                       sy1[:, i:i + 1], b1b[:, fb * 512:fb * 512 + 512],
                                                       OP.mult, OP.add)
                                vcnt3.append(k.c["V"])
                            k.on("vector", vv)

                            def sg(s, u=u, i=i, fb=fb):
                                s.wait("V", vcnt3[u])
                                if u >= 2:
                                    s.wait("dA%d" % (u % 2), ycnt3[u - 2])
                                s.activation(out=hb[:, (u % 2) * 512:(u % 2) * 512 + 512],
                                             in_=tbv[:, (u % 2) * 512:(u % 2) * 512 + 512], func=AF.Gelu)
                                scnt3.append(k.c["S"])
                            k.on("scalar", sg)

                            def yh(y, u=u, i=i, fb=fb):
                                y.wait("S", scnt3[u])
                                k.inc(y.dma_start(out=d_h[i * 128:(i + 1) * 128, fb * 512:(fb + 1) * 512],
                                                  in_=hb[:, (u % 2) * 512:(u % 2) * 512 + 512]), "dA%d" % (u % 2))
                                ycnt3.append(k.c["dA%d" % (u % 2)])
                            k.on("sync", yh)
                    k.barrier()

                with nc.sbuf_tensor("w2s", [128, 32 * 1024], BF16) as w2s_t, \
                     nc.sbuf_tensor("hf", [128, 4096], BF16) as hf_t, \
                     nc.sbuf_tensor("hq", [128, 2048], F32) as hq_t, \
                     nc.sbuf_tensor("xq2", [128, 4096], BF16) as xq2_t, \
                     nc.sbuf_tensor("x2T", [128, 4096], BF16) as x2T_t, \
                     nc.sbuf_tensor("b2b", [128, 1024], BF16) as b2b_t, \
                     nc.sbuf_tensor("b2r", [1, 1024], BF16) as b2r_t, \
                     nc.sbuf_tensor("tb8", [128, 2 * 512], F32) as tb8_t, \
                     nc.psum_tensor("ptq2", [128, 2048], BF16) as ptq2_t, \
                     nc.psum_tensor("py2", [128, 4 * 512], F32) as py2_t:
                    w2s, hf, hq, xq2, x2T = w2s_t[:, :], hf_t[:, :], hq_t[:, :], xq2_t[:, :], x2T_t[:, :]
                    b2b, b2r, tb8, ptq2, py2 = b2b_t[:, :], b2r_t[:, :], tb8_t[:, :], ptq2_t[:, :], py2_t[:, :]

                    def yw2(y):
                        for c in range(32):
                            k.inc(y.dma_start(out=w2s[:, c * 1024:(c + 1) * 1024],
                                              in_=d_w2q[c * 128:(c + 1) * 128, :]), "dY")
                        k.inc(y.dma_start(out=tb8[0:1, 0:1024], in_=I["w2b"][0:1, :]), "dY")
                    k.on("sync", yw2)

                    def gw2(g):
                        g.wait("dY")
                        g.tensor_copy(b2r[0:1, :], tb8[0:1, 0:1024])
                    k.on("gpsimd", gw2)

                    def gw2t(t):
                        t.wait("G")
                        for b2i in range(2):
                            t.matmul(py2[:, b2i * 512:(b2i + 1) * 512], onesb[0:1, 0:128],
                                     b2r[0:1, b2i * 512:(b2i + 1) * 512], start=True, stop=True)
                    k.on("tensor", gw2t)

                    def gw2v(v):
                        v.wait("T")
                        v.tensor_copy(b2b[:, 0:1024], py2[:, 0:1024])
                    k.on("vector", gw2v)
                    k.barrier()

                    dch, scn, vcn, tcn, ecn, rcn, x2c = [], [], [], [], [], [], []
                    for i in range(TT):
                        def yh2(y, i=i):
                            if i >= 1:
                                y.wait("V", vcn[2 * i - 1])
                            k.inc(y.dma_start(out=hf[:, 0:4096],
                                              in_=d_h[i * 128:(i + 1) * 128, :]), "dY%d" % (i % 2))
                            dch.append(k.c["dY%d" % (i % 2)])
                        k.on("sync", yh2)

                        def s2(s, i=i):
                            s.wait("dY%d" % (i % 2), dch[i])
                            s.wait("S")
                            s.wait("T")
                            s.activation(out=xq2[:, 0:4096],
                                         in_=hf[:, 0:4096],
                                         func=AF.Square, accum_out=st2[:, 8:9])
                            scn.append(k.c["S"])
                        k.on("scalar", s2)

                        def v2(v, i=i):
                            v.wait("S", scn[-1] if True else 0)
                            v.wait("V")
                            hsl = hf[:, 0:4096]
                            v.reduce_max(st2[:, 9:10], hsl, axis=AX, apply_absolute_value=True)
                            v.tensor_scalar(st2[:, 10:11], st2[:, 8:9], 1.0 / 4096.0, 1e-5, OP.mult, OP.add)
                            vcn.append(k.c["V"])
                        k.on("vector", v2)

                        def s3(s, i=i):
                            s.wait("V", vcn[-1])
                            s.activation(out=st2[:, 11:12], in_=st2[:, 10:11], func=AF.Sqrt)
                            scn.append(k.c["S"])
                        k.on("scalar", s3)

                        def v3(v, i=i):
                            v.wait("S", scn[-1])
                            v.reciprocal(st2[:, 12:13], st2[:, 11:12])
                            v.wait("V")
                            v.tensor_scalar(st2[:, 13:14], st2[:, 9:10], st2[:, 12:13], 1e-5, OP.mult, OP.max)
                            v.wait("V")
                            v.reciprocal(st2[:, 14:15], st2[:, 13:14])
                            v.tensor_scalar(sy1[:, i:i + 1], st2[:, 13:14], small[:, 9:10], None, OP.mult)
                            v.wait("V")
                            v.tensor_scalar(st2[:, 14:15], st2[:, 14:15], 127.0, None, OP.mult)
                            v.tensor_scalar(sy1[:, i:i + 1], sy1[:, i:i + 1], 1.0 / 127.0, None, OP.mult)
                            v.wait("V")
                            v.tensor_tensor(st2[:, 15:16], st2[:, 12:13], st2[:, 14:15], OP.mult)
                            v.wait("V")
                            hsl = hf[:, 0:4096]
                            for hh in range(2):
                                v.tensor_scalar(hq[:, 0:2048], hsl[:, hh * 2048:(hh + 1) * 2048],
                                                st2[:, 15:16], 12582912.0, OP.mult, OP.add)
                                v.wait("V")
                                v.tensor_scalar(xq2[:, hh * 2048:hh * 2048 + 2048], hq[:, 0:2048],
                                                12582912.0, None, OP.subtract)
                                v.wait("V")
                            vcn.append(k.c["V"])
                        k.on("vector", v3)

                        for half in range(2):
                            def t2(t, i=i, half=half):
                                t.wait("V")
                                if i >= 2 and half == 0:
                                    t.wait("V", ecn[2 * (i - 2) + 1])
                                for c in range(16):
                                    t.transpose(ptq2[:, half * 1024 + ((c % 8)) * 128:half * 1024 + ((c % 8)) * 128 + 128]
                                                if False else
                                                ptq2[:, (c % 16) * 128:(c % 16) * 128 + 128],
                                                xq2[:, half * 2048 + c * 128:half * 2048 + c * 128 + 128],
                                                identb[:, :])
                                tcn.append(k.c["T"])
                            k.on("tensor", t2)

                            def ve2(v, i=i, half=half):
                                v.wait("T", tcn[-1])
                                if i >= 1:
                                    v.wait("T", x2c[2 * (i - 1) + 1])
                                v.wait("V")
                                v.tensor_copy(x2T[:, half * 2048:half * 2048 + 2048],
                                              ptq2[:, 0:2048])
                                ecn.append(k.c["V"])
                            k.on("vector", ve2)

                        for nb in range(2):
                            def t3(t, i=i, nb=nb):
                                u = 2 * i + nb
                                t.wait("V", ecn[-1])
                                if u >= 4:
                                    t.wait("V", rcn[u - 4])
                                bank = py2[:, (u % 4) * 512:(u % 4) * 512 + 512]
                                for c in range(32):
                                    t.matmul(bank, x2T[:, c * 128:c * 128 + 128],
                                             w2s[:, c * 1024 + nb * 512: c * 1024 + nb * 512 + 512],
                                             start=(c == 0), stop=(c == 31))
                                tcn.append(k.c["T"])
                                x2c.append(k.c["T"])
                            k.on("tensor", t3)

                            def v4(v, i=i, nb=nb):
                                u = 2 * i + nb
                                v.wait("T", tcn[-1])
                                bank = py2[:, (u % 4) * 512:(u % 4) * 512 + 512]
                                rsl = R[:, i * 1024 + nb * 512:i * 1024 + nb * 512 + 512]
                                v.wait("V")
                                v.scalar_tensor_tensor(tb8[:, (u % 2) * 512:(u % 2) * 512 + 512], bank,
                                                       sy1[:, i:i + 1], b2b[:, nb * 512:nb * 512 + 512],
                                                       OP.mult, OP.add)
                                v.wait("V")
                                v.tensor_tensor(rsl, rsl, tb8[:, (u % 2) * 512:(u % 2) * 512 + 512], OP.add)
                                rcn.append(k.c["V"])
                            k.on("vector", v4)
                    k.barrier()

        def l1aa_p2_chunk(ic, C):
            dci, tci, vci, sci, yci = {}, {}, {}, {}, {}

            def ybd(y):
                if ic >= 2:
                    y.wait("G", C.phG[ic - 2])
                for w3, nm in enumerate(("bdq", "bdk", "bdv")):
                    k.inc(y.dma_start(out=C.bds[:, (ic % 2) * 384 + w3 * 128:(ic % 2) * 384 + w3 * 128 + 128],
                                      in_=I[nm][ic]), "dY%d" % (ic % 2))
                k.inc(y.dma_start(out=C.cwc[:, (ic % 2) * 8:(ic % 2) * 8 + 4],
                                  in_=I["convw"][ic * 128:(ic + 1) * 128, :]), "dY%d" % (ic % 2))
                k.inc(y.dma_start(out=C.cwc[:, (ic % 2) * 8 + 4:(ic % 2) * 8 + 5],
                                  in_=I["convb"][ic * 128:(ic + 1) * 128, :]), "dY%d" % (ic % 2))
                dci["bd"] = k.c["dY%d" % (ic % 2)]
            k.on("sync", ybd)

            def gbd(g):
                g.wait("dY%d" % (ic % 2), dci["bd"])
                g.wait("G")
                g.tensor_copy(C.bdc[:, (ic % 2) * 384:(ic % 2) * 384 + 384],
                              C.bds[:, (ic % 2) * 384:(ic % 2) * 384 + 384])
                C.phG.append(k.c["G"])
            k.on("gpsimd", gbd)

            def txm(t, tb):
                t.wait("V")
                bank = C.pxm[:, (tb % 2) * 512:(tb % 2) * 512 + 512]
                for c in range(FC):
                    t.matmul(bank, C.wm[:, c * 2048 + ic * 128: c * 2048 + ic * 128 + 128],
                             xlnT[:, c * 2048 + tb * 512: c * 2048 + tb * 512 + 512],
                             start=(c == 0), stop=(c == FC - 1))
                tci[("xm", tb)] = k.c["T"]

            def vxm(v, tb):
                v.wait("T", tci[("xm", tb)])
                v.tensor_copy(C.xmb[:, 3 + tb * 512:3 + tb * 512 + 512],
                              C.pxm[:, (tb % 2) * 512:(tb % 2) * 512 + 512])
                vci[("xm", tb)] = k.c["V"]
            for tb in range(4):
                k.on("tensor", (lambda tb: lambda t: txm(t, tb))(tb))
                k.on("vector", (lambda tb: lambda v: vxm(v, tb))(tb))

            def vcv(v):
                v.wait("V")
                v.wait("dY%d" % (ic % 2), dci["bd"])
                cw = C.cwc[:, (ic % 2) * 8:(ic % 2) * 8 + 8]
                v.tensor_scalar(C.acc[:, :], C.xmb[:, 0:2048], cw[:, 0:1], None, OP.mult)
                for j in (1, 2, 3):
                    v.wait("V")
                    v.scalar_tensor_tensor(C.acc[:, :], C.xmb[:, j:j + 2048], cw[:, j:j + 1],
                                           C.acc[:, :], OP.mult, OP.add)
                vci["cv"] = k.c["V"]
            k.on("vector", vcv)

            def scv(s):
                s.wait("V", vci["cv"])
                s.wait("S")
                s.activation(out=C.xcb[:, :], in_=C.acc[:, :], func=AF.Silu,
                             bias=C.cwc[:, (ic % 2) * 8 + 4:(ic % 2) * 8 + 5])
                sci["xc"] = k.c["S"]
            k.on("scalar", scv)

            def tqk(t, w3, srcsel, tb):
                if srcsel == "xc":
                    t.wait("S", sci["xc"])
                else:
                    t.wait("V", vci[("xm", 3)])
                t.wait("V")
                t.wait("G")
                bank = C.pxm[:, (tb % 2) * 512:(tb % 2) * 512 + 512]
                rhs = (C.xcb[:, tb * 512:(tb + 1) * 512] if srcsel == "xc"
                       else C.xmb[:, 3 + tb * 512:3 + tb * 512 + 512])
                t.matmul(bank, C.bdc[:, (ic % 2) * 384 + w3 * 128:(ic % 2) * 384 + w3 * 128 + 128],
                         rhs, start=True, stop=True)
                tci[(w3, tb)] = k.c["T"]

            def vqk(v, w3, tb):
                v.wait("T", tci[(w3, tb)])
                v.tensor_copy(C.qkvc[:, w3 * 2048 + tb * 512: w3 * 2048 + tb * 512 + 512],
                              C.pxm[:, (tb % 2) * 512:(tb % 2) * 512 + 512])
                vci[(w3, tb)] = k.c["V"]
            for w3, srcsel in ((0, "xc"), (1, "xc"), (2, "xm")):
                for tb in range(4):
                    k.on("tensor", (lambda w3, srcsel, tb: lambda t: tqk(t, w3, srcsel, tb))(w3, srcsel, tb))
                    k.on("vector", (lambda w3, tb: lambda v: vqk(v, w3, tb))(w3, tb))

            for gi in range(2):
                def tig(t, gi=gi):
                    t.wait("V")
                    for w3 in range(3):
                        lh = C.wig[:, (w3 * 16 + ic) * 8 + gi * 4:(w3 * 16 + ic) * 8 + gi * 4 + 4]
                        for tb in range(4):
                            t.matmul(C.pig[0:4, tb * 512:(tb + 1) * 512], lh,
                                     C.qkvc[:, w3 * 2048 + tb * 512: w3 * 2048 + tb * 512 + 512],
                                     start=(w3 == 0), stop=(w3 == 2))
                    tci[("ig", gi)] = k.c["T"]
                k.on("tensor", tig)

                def vig(v, gi=gi):
                    v.wait("T", tci[("ig", gi)])
                    v.wait("V")
                    dstg = C.A0 if gi == 0 else C.A1
                    v.scalar_tensor_tensor(dstg, C.pig[0:4, :], 0.0, dstg, OP.add, OP.add)
                k.on("vector", vig)

            def yqk(y):
                y.wait("V", vci[(1, 3)])
                y.wait("T")
                k.inc(y.dma_start(out=d_q[ic * 128:(ic + 1) * 128, :], in_=C.qkvc[:, 0:2048]), "dA0")
                k.inc(y.dma_start(out=d_k[ic * 128:(ic + 1) * 128, :], in_=C.qkvc[:, 2048:4096]), "dA0")
                yci["qk"] = k.c["dA0"]
            k.on("sync", yqk)

            def ttp5(t, nm):
                if nm == "xct":
                    t.wait("S", sci["xc"])
                else:
                    t.wait("V", vci[(2, 3)])
                t.wait("V")
                srcb = C.xcb if nm == "xct" else C.qkvc[:, 2 * 2048:3 * 2048]
                for tt2 in range(16):
                    t.transpose(C.ptp5[:, tt2 * 128:(tt2 + 1) * 128],
                                srcb[:, tt2 * 128:(tt2 + 1) * 128], identb[:, :])
                tci[nm] = k.c["T"]

            def vtp5(v, nm):
                v.wait("T", tci[nm])
                v.wait("V")
                v.tensor_copy(C.tpe[:, :], C.ptp5[:, :])
                vci[nm] = k.c["V"]

            def ytp5(y, nm):
                y.wait("V", vci[nm])
                dst = (d_xct if nm == "xct" else d_vt).rearrange(
                    "(tt p) c -> p tt c", p=128)[:, :, ic * 128:(ic + 1) * 128]
                k.inc(y.dma_start(out=dst, in_=C.tpe_t.ap().rearrange("p (tt c) -> p tt c", tt=16)), "dA1")
                yci[nm] = k.c["dA1"]

            for nm in ("xct", "vt"):
                k.on("tensor", (lambda nm: lambda t: ttp5(t, nm))(nm))
                k.on("vector", (lambda nm: lambda v: vtp5(v, nm))(nm))
                k.on("sync", (lambda nm: lambda y: ytp5(y, nm))(nm))
                k.on("vector", (lambda nm: lambda v: v.wait("dA1", yci[nm]))(nm))
            k.on("vector", lambda v: v.wait("dA0", yci["qk"]))

        def l1aa_p4_head(sb, h, C):
            dc5, tc5, vc5, sc5, gc5 = {}, {}, {}, {}, {}
            ntc = 4 * sb + 4

            def yld(y):
                y.wait("T")
                y.wait("V")
                y.wait("S")
                for c4 in range(4):
                    k.inc(y.dma_start(out=C.qh[:, c4 * 2048:(c4 + 1) * 2048],
                                      in_=d_q[h * 512 + c4 * 128:h * 512 + (c4 + 1) * 128, :]), "dY0")
                    k.inc(y.dma_start(out=C.kh[:, c4 * 2048:(c4 + 1) * 2048],
                                      in_=d_k[h * 512 + c4 * 128:h * 512 + (c4 + 1) * 128, :]), "dY0")
                for tt2 in range(ntc):
                    k.inc(y.dma_start(out=C.vh[:, tt2 * 512:(tt2 + 1) * 512],
                                      in_=d_vt[tt2 * 128:(tt2 + 1) * 128, h * 512:(h + 1) * 512]), "dY0")
                k.inc(y.dma_start(out=C.mr5[0:1, :], in_=C.msrc[h:h + 1, :]), "dY0")
                k.inc(y.dma_start(out=C.er5[0:1, :], in_=C.esrc[h:h + 1, :]), "dY0")
                for sc in range(4):
                    k.inc(y.dma_start(out=C.xzs[:, sc * 512:(sc + 1) * 512],
                                      in_=d_xct[(4 * sb + sc) * 128:(4 * sb + sc + 1) * 128,
                                                h * 512:(h + 1) * 512]), "dY0")
                    k.inc(y.dma_start(out=C.xzs[:, 2048 + sc * 512:2048 + (sc + 1) * 512],
                                      in_=d_z[(4 * sb + sc) * 128:(4 * sb + sc + 1) * 128,
                                              h * 512:(h + 1) * 512]), "dY0")
                dc5["ld"] = k.c["dY0"]
            k.on("sync", yld)

            def vsc5(v):
                v.wait("dY0", dc5["ld"])
                for c4 in range(4):
                    v.wait("V")
                    v.tensor_scalar(C.qh[:, c4 * 2048:c4 * 2048 + 2048],
                                    C.qh[:, c4 * 2048:c4 * 2048 + 2048],
                                    1.0 / 22.627416997969522, None, OP.mult)
                vc5["qs"] = k.c["V"]
            k.on("vector", vsc5)

            def gmb(t):
                t.wait("dY0", dc5["ld"])
                t.wait("V")
                t.matmul(C.pqk5[:, 0:512], onesf[0:1, :], C.mr5[0:1, sb * 512:sb * 512 + 512],
                         start=True, stop=True)
                gc5["mbt"] = k.c["T"]
            k.on("tensor", gmb)

            def gmbv(v):
                v.wait("T", gc5["mbt"])
                v.tensor_copy(C.mb5[:, :], C.pqk5[:, 0:512])
                gc5["mb"] = k.c["V"]
            k.on("vector", gmbv)

            def tqk5(t, tc):
                t.wait("V", vc5["qs"])
                if tc < 2:
                    t.wait("V")
                if tc >= 2:
                    t.wait("V", vc5[("ct", tc - 2)])
                bank = C.pqk5[:, (tc % 2) * 512:(tc % 2) * 512 + 512]
                for c4 in range(4):
                    t.matmul(bank, C.kh[:, c4 * 2048 + tc * 128: c4 * 2048 + tc * 128 + 128],
                             C.qh[:, c4 * 2048 + sb * 512: c4 * 2048 + sb * 512 + 512],
                             start=(c4 == 0), stop=(c4 == 3))
                tc5[("qk", tc)] = k.c["T"]

            def sE5(s, tc):
                s.wait("V", gc5["mb"])
                if tc >= 2:
                    s.wait("V", vc5[("ct", tc - 2)])
                s.wait("S")
                s.activation(out=C.Et5[:, (tc % 2) * 512:(tc % 2) * 512 + 512],
                             in_=C.mb5[:, 0:512], func=AF.Exp,
                             scale=-1.0, bias=C.gcol[:, tc * 4 + h:tc * 4 + h + 1])
                sc5[("e", tc)] = k.c["S"]

            def vC5(v, tc):
                v.wait("T", tc5[("qk", tc)])
                v.wait("S", sc5[("e", tc)])
                v.wait("V")
                v.scalar_tensor_tensor(C.Ct5[:, (tc % 2) * 512:(tc % 2) * 512 + 512],
                                       C.Et5[:, (tc % 2) * 512:(tc % 2) * 512 + 512], 1.0,
                                       C.pqk5[:, (tc % 2) * 512:(tc % 2) * 512 + 512],
                                       OP.min, OP.mult)
                if tc >= 4 * sb:
                    v.wait("V")
                    v.tensor_tensor(C.Ct5[:, (tc % 2) * 512:(tc % 2) * 512 + 512],
                                    C.Ct5[:, (tc % 2) * 512:(tc % 2) * 512 + 512],
                                    C.trilb[:, (tc - 4 * sb) * 512:(tc - 4 * sb) * 512 + 512],
                                    OP.mult)
                vc5[("ct", tc)] = k.c["V"]

            def tpv5(t, tc):
                t.wait("V", vc5[("ct", tc)])
                for dhc in range(4):
                    t.matmul(C.pcx5[:, dhc * 512:(dhc + 1) * 512],
                             C.vh[:, tc * 512 + dhc * 128: tc * 512 + dhc * 128 + 128],
                             C.Ct5[:, (tc % 2) * 512:(tc % 2) * 512 + 512],
                             start=(tc == 0), stop=(tc == ntc - 1))
                t.matmul(C.prs5[0:1, :], C.onc[:, 0:1],
                         C.Ct5[:, (tc % 2) * 512:(tc % 2) * 512 + 512],
                         start=(tc == 0), stop=(tc == ntc - 1))
                tc5[("pv", tc)] = k.c["T"]
            for tc in range(ntc):
                k.on("tensor", (lambda tc: lambda t: tqk5(t, tc))(tc))
                k.on("scalar", (lambda tc: lambda s: sE5(s, tc))(tc))
                k.on("vector", (lambda tc: lambda v: vC5(v, tc))(tc))
                k.on("tensor", (lambda tc: lambda t: tpv5(t, tc))(tc))

            def vN5(v):
                v.wait("T", tc5[("pv", ntc - 1)])
                v.wait("V")
                v.tensor_copy(C.nrm[0:1, 0:512], C.prs5[0:1, :])
                v.wait("V")
                v.tensor_scalar(C.nrb[0:1, 0:512], C.nrm[0:1, 0:512], -1.0, None, OP.mult)
                v.wait("V")
                v.tensor_tensor(C.nrm[0:1, 0:512], C.nrm[0:1, 0:512], C.nrb[0:1, 0:512], OP.max)
                v.wait("V")
                v.tensor_tensor(C.nrm[0:1, 0:512], C.nrm[0:1, 0:512],
                                C.er5[0:1, sb * 512:sb * 512 + 512], OP.max)
                v.wait("V")
                v.tensor_scalar(C.nrm[0:1, 0:512], C.nrm[0:1, 0:512], 1e-6, None, OP.add)
                v.wait("V")
                v.reciprocal(C.nrm[0:1, 0:512], C.nrm[0:1, 0:512])
                vc5["nr"] = k.c["V"]
            k.on("vector", vN5)

            def gN5(t):
                t.wait("V", vc5["nr"])
                t.matmul(C.pqk5[:, 512:1024], onesf[0:1, :], C.nrm[0:1, 0:512],
                         start=True, stop=True)
                gc5["nbt"] = k.c["T"]
            k.on("tensor", gN5)

            def gN5v(v):
                v.wait("T", gc5["nbt"])
                v.tensor_copy(C.nrb[:, :], C.pqk5[:, 512:1024])
                gc5["nb"] = k.c["V"]
            k.on("vector", gN5v)

            def vE5(v):
                v.wait("V", gc5["nb"])
                for dhc in range(4):
                    v.wait("V")
                    v.tensor_tensor(C.cnb[:, dhc * 512:(dhc + 1) * 512],
                                    C.pcx5[:, dhc * 512:(dhc + 1) * 512], C.nrb[:, :], OP.mult)
                vc5["cn"] = k.c["V"]
            k.on("vector", vE5)

            def tT5(t, half):
                t.wait("V", vc5["cn"] if half == 0 else vc5[("htc", 0)])
                for dhc in (2 * half, 2 * half + 1):
                    for sc in range(4):
                        t.transpose(C.ptx5[:, (sc % 2) * 512 + (dhc % 2) * 0 + 0:0] if False else
                                    C.ptx5[:, ((dhc % 2) * 4 + sc) * 128: ((dhc % 2) * 4 + sc) * 128 + 128],
                                    C.cnb[:, dhc * 512 + sc * 128: dhc * 512 + sc * 128 + 128],
                                    identb[:, :])
                tc5[("tx", half)] = k.c["T"]

            def vH5(v, half):
                v.wait("T", tc5[("tx", half)])
                v.wait("V")
                src5 = C.ptx5[:, 0:1024].rearrange("p (d s c) -> p d s c", d=2, s=4)
                dst5 = C.hts.rearrange("p (s d c) -> p s d c", s=4, d=4)[:, :, 2 * half:2 * half + 2, :]
                v.tensor_copy(dst5.rearrange("p s d c -> p d s c"), src5)
                vc5[("htc", half)] = k.c["V"]
            for half in (0, 1):
                k.on("tensor", (lambda half: lambda t: tT5(t, half))(half))
                k.on("vector", (lambda half: lambda v: vH5(v, half))(half))

            def vH5b(v):
                vc5["ht"] = vc5[("htc", 1)]
            vH5b(None)

            def sS5(s, sc):
                s.wait("V", vc5["ht"])
                s.wait("V")
                s.wait("S")
                s.activation(out=C.Et5[:, 0:512], in_=C.hts[:, sc * 512:(sc + 1) * 512],
                             func=AF.Square, accum_out=C.st5[:, 0:1])
                sc5[("s2", sc)] = k.c["S"]

            def vS5(v, sc):
                v.wait("S", sc5[("s2", sc)])
                v.wait("V")
                v.reduce_sum(C.st5[:, 1:2], C.hts[:, sc * 512:(sc + 1) * 512], axis=AX)
                v.wait("V")
                v.tensor_scalar(C.st5[:, 1:2], C.st5[:, 1:2], 1.0 / 512.0, None, OP.mult)
                v.tensor_scalar(C.st5[:, 2:3], C.st5[:, 0:1], 1.0 / 512.0, None, OP.mult)
                v.wait("V")
                v.tensor_tensor(C.st5[:, 3:4], C.st5[:, 1:2], C.st5[:, 1:2], OP.mult)
                v.wait("V")
                v.tensor_tensor(C.st5[:, 2:3], C.st5[:, 2:3], C.st5[:, 3:4], OP.subtract)
                vc5[("st", sc)] = k.c["V"]

            def sR5(s, sc):
                s.wait("V", vc5[("st", sc)])
                s.activation(out=C.st5[:, 2:3], in_=C.st5[:, 2:3], func=AF.Sqrt, bias=small[:, 7:8])
                sc5[("sq", sc)] = k.c["S"]

            def vR5(v, sc):
                v.wait("S", sc5[("sq", sc)])
                v.reciprocal(C.st5[:, 4:5], C.st5[:, 2:3])
                v.wait("V")
                v.tensor_tensor(C.st5[:, 5:6], C.st5[:, 1:2], C.st5[:, 4:5], OP.mult)
                v.wait("V")
                v.tensor_scalar(C.st5[:, 5:6], C.st5[:, 5:6], -1.0, None, OP.mult)
                vc5[("rs", sc)] = k.c["V"]

            def sA5(s, sc):
                s.wait("V", vc5[("rs", sc)])
                s.activation(out=C.Et5[:, 0:512], in_=C.hts[:, sc * 512:(sc + 1) * 512],
                             func=AF.Identity, scale=C.st5[:, 4:5], bias=C.st5[:, 5:6])
                sc5[("hn", sc)] = k.c["S"]

            def vT5(v, sc):
                v.wait("S", sc5[("hn", sc)])
                v.wait("V")
                v.tensor_tensor(C.Et5[:, 512:1024], C.Et5[:, 0:512],
                                C.lsk[:, h * 512:(h + 1) * 512], OP.mult)
                v.tensor_tensor(C.Ct5[:, 0:512], C.xzs[:, sc * 512:(sc + 1) * 512],
                                C.lsk[:, 2048 + h * 512:2048 + (h + 1) * 512], OP.mult)
                v.wait("V")
                v.tensor_tensor(C.Et5[:, 512:1024], C.Et5[:, 512:1024], C.Ct5[:, 0:512], OP.add)
                v.wait("V")
                v.tensor_tensor(C.Ct5[:, 512:1024], C.Et5[:, 512:1024],
                                C.xzs[:, 2048 + sc * 512:2048 + (sc + 1) * 512], OP.mult)
                vc5[("hs", sc)] = k.c["V"]

            def tH5(t, sc):
                t.wait("V", vc5[("hs", sc)])
                for dhc in range(4):
                    t.transpose(C.ptx5[:, dhc * 128:(dhc + 1) * 128],
                                C.Ct5[:, 512 + dhc * 128:512 + (dhc + 1) * 128], identb[:, :])
                tc5[("th", sc)] = k.c["T"]

            def vF5(v, sc):
                v.wait("T", tc5[("th", sc)])
                v.wait("V")
                dst = C.hsT_t.ap().rearrange("p (ic s) -> p ic s", ic=16)[
                    :, 4 * h:4 * h + 4, sc * 128:(sc + 1) * 128]
                v.tensor_copy(dst, C.ptx5[:, 0:512].rearrange("p (a b) -> p a b", a=4))
                C.hs_done[(h, sc)] = k.c["V"]
            for sc in range(4):
                k.on("scalar", (lambda sc: lambda s: sS5(s, sc))(sc))
                k.on("vector", (lambda sc: lambda v: vS5(v, sc))(sc))
                k.on("scalar", (lambda sc: lambda s: sR5(s, sc))(sc))
                k.on("vector", (lambda sc: lambda v: vR5(v, sc))(sc))
                k.on("scalar", (lambda sc: lambda s: sA5(s, sc))(sc))
                k.on("vector", (lambda sc: lambda v: vT5(v, sc))(sc))
                k.on("tensor", (lambda sc: lambda t: tH5(t, sc))(sc))
                k.on("vector", (lambda sc: lambda v: vF5(v, sc))(sc))

        def l1aa_p4_down(sb, C):
            for nb in range(2):
                load_cast(C.stgd, [(I["downT"][:, nb * 512:(nb + 1) * 512],
                                    lambda c: C.dwn[:, c * 512:(c + 1) * 512], 2048, 512)],
                          per=512, after=("T", k.c["T"]))

                def tdw(t, nb, sc):
                    t.wait("G")
                    t.wait("V", C.hs_done[(3, 3)])
                    if len(C.dwv) >= 2:
                        t.wait("V", C.dwv[-2])
                    bank = C.pqk5[:, (sc % 2) * 512:(sc % 2) * 512 + 512]
                    for c in range(16):
                        t.matmul(bank, C.hsT[:, c * 512 + sc * 128: c * 512 + sc * 128 + 128],
                                 C.dwn[:, c * 512:(c + 1) * 512], start=(c == 0), stop=(c == 15))
                    C.tdwc[(nb, sc)] = k.c["T"]

                def vdw(v, nb, sc):
                    v.wait("T", C.tdwc[(nb, sc)])
                    v.wait("V")
                    bank = C.pqk5[:, (sc % 2) * 512:(sc % 2) * 512 + 512]
                    rsl = R[:, (4 * sb + sc) * 1024 + nb * 512:(4 * sb + sc) * 1024 + nb * 512 + 512]
                    v.scalar_tensor_tensor(rsl, bank, 0.0, rsl, OP.add, OP.add)
                    C.dwv.append(k.c["V"])
                for sc in range(4):
                    k.on("tensor", (lambda nb, sc: lambda t: tdw(t, nb, sc))(nb, sc))
                    k.on("vector", (lambda nb, sc: lambda v: vdw(v, nb, sc))(nb, sc))

        def sub_l1aa():
            import types
            C = types.SimpleNamespace()
            with nc.psum_tensor("psb", [128, 2048], BF16) as psb:
                emit_ln(psb[:, :])
            k.barrier()
            with nc.sbuf_tensor("wz", [128, FC * 2048], BF16) as wz_t, \
                 nc.sbuf_tensor("stgz", [128, 2 * 2048], F32) as stgz_t, \
                 nc.sbuf_tensor("zb", [128, 2 * 512], BF16) as zb_t, \
                 nc.psum_tensor("pz", [128, 4 * 512], F32) as pz_t:
                wz, zb, pz = wz_t[:, :], zb_t[:, :], pz_t[:, :]
                load_cast(stgz_t[:, :], [(I["upT"][:, 2048:4096], lambda c: wz[:, c * 2048:(c + 1) * 2048], D, 2048)])
                k.barrier()
                tz, vz2, yz = [], [], []
                for u in range(TT * 4):
                    i, zc = u // 4, u % 4

                    def tm(t, u=u, i=i, zc=zc):
                        if u >= 4:
                            t.wait("S", vz2[u - 4])
                        bank = pz[:, (u % 4) * 512:(u % 4) * 512 + 512]
                        for c in range(FC):
                            t.matmul(bank, xlnT[:, c * 2048 + i * 128: c * 2048 + i * 128 + 128],
                                     wz[:, c * 2048 + zc * 512: c * 2048 + zc * 512 + 512],
                                     start=(c == 0), stop=(c == FC - 1))
                        tz.append(k.c["T"])
                    k.on("tensor", tm)

                    def sv(s, u=u):
                        s.wait("T", tz[u])
                        if u >= 2:
                            s.wait("dA%d" % (u % 2), yz[u - 2])
                        s.activation(out=zb[:, (u % 2) * 512:(u % 2) * 512 + 512],
                                     in_=pz[:, (u % 4) * 512:(u % 4) * 512 + 512], func=AF.Silu)
                        vz2.append(k.c["S"])
                    k.on("scalar", sv)

                    def yp(y, u=u, i=i, zc=zc):
                        y.wait("S", vz2[u])
                        k.inc(y.dma_start(out=d_z[i * 128:(i + 1) * 128, zc * 512:(zc + 1) * 512],
                                          in_=zb[:, (u % 2) * 512:(u % 2) * 512 + 512]), "dA%d" % (u % 2))
                        yz.append(k.c["dA%d" % (u % 2)])
                    k.on("sync", yp)
                k.barrier()

            aas = ExitStack()
            stgm_t = aas.enter_context(nc.sbuf_tensor("stgm", [128, 2 * 2048], F32))
            gcol_t = aas.enter_context(nc.sbuf_tensor("gcol", [128, 64], F32))
            bgc_t = aas.enter_context(nc.sbuf_tensor("bgc", [4, 2], F32))
            p2s = ExitStack()
            if True:
                en2 = p2s.enter_context
                wm_t = en2(nc.sbuf_tensor("wm", [128, FC * 2048], BF16))
                wig_t = en2(nc.sbuf_tensor("wig", [128, 48 * 8], BF16))
                xmb_t = en2(nc.sbuf_tensor("xmb", [128, 2052], BF16))
                xcb_t = en2(nc.sbuf_tensor("xcb", [128, 2048], BF16))
                acc_t = en2(nc.sbuf_tensor("acc", [128, 2048], F32))
                qkvc_t = en2(nc.sbuf_tensor("qkvc", [128, 3 * 2048], BF16))
                tpe_t = en2(nc.sbuf_tensor("tpe", [128, 2048], BF16))
                bdc_t = en2(nc.sbuf_tensor("bdc", [128, 2 * 384], BF16))
                bds_t = en2(nc.sbuf_tensor("bds", [128, 2 * 384], F32))
                cwc_t = en2(nc.sbuf_tensor("cwc", [128, 2 * 8], F32))
                pig_t = en2(nc.psum_tensor("pig", [4, 2048], F32))
                pxm_t = en2(nc.psum_tensor("pxm", [128, 2 * 512], F32))
                ptp5_t = en2(nc.psum_tensor("ptp5", [128, 2048], BF16))
                C.wm, C.wig, C.xmb, C.xcb, C.acc = wm_t[:, :], wig_t[:, :], xmb_t[:, :], xcb_t[:, :], acc_t[:, :]
                C.qkvc, C.tpe, C.tpe_t, C.bdc, C.bds, C.cwc = (qkvc_t[:, :], tpe_t[:, :], tpe_t,
                                                               bdc_t[:, :], bds_t[:, :], cwc_t[:, :])
                C.A0 = stgm_t[0:4, 0:2048]
                C.A1 = stgm_t[0:4, 2048:4096]
                C.gcol, C.bgc = gcol_t[:, :], bgc_t[:, :]
                C.pig, C.pxm, C.ptp5 = pig_t[:, :], pxm_t[:, :], ptp5_t[:, :]
                C.phG = []
                load_cast(stgm_t[:, :], [(I["upT"][:, 0:2048], lambda c: C.wm[:, c * 2048:(c + 1) * 2048], D, 2048)])
                load_cast(stgm_t[:, :], [(I["wigfgT"], lambda c: C.wig[:, c * 8:(c + 1) * 8], 6144, 8)])

                def yg(y):
                    k.inc(y.dma_start(out=C.bgc[:, :], in_=I["bigfg"][:, :]), "dY")
                k.on("sync", yg)

                def vz4(v):
                    v.memset(C.xmb[:, 0:3], 0.0)
                k.on("vector", vz4)
                k.barrier()

                def vz5(v):
                    v.memset(C.A0, 0.0)
                    v.memset(C.A1, 0.0)
                k.on("vector", vz5)
                k.barrier()
                for ic in range(16):
                    l1aa_p2_chunk(ic, C)
                k.barrier()

                def vg1(v):
                    v.wait("V")
                    v.tensor_scalar(C.A0, C.A0, C.bgc[:, 0:1], None, OP.add)
                    v.tensor_scalar(C.A1, C.A1, C.bgc[:, 1:2], None, OP.add)
                k.on("vector", vg1)

                def sg1(s):
                    A2 = C.acc[0:4, 0:2048]
                    s.wait("V")
                    s.activation(out=A2, in_=C.A1, func=AF.Abs)
                    s.wait("S")
                    s.activation(out=A2, in_=A2, func=AF.Exp, scale=-1.0)
                    s.wait("S")
                    s.activation(out=A2, in_=A2, func=AF.Ln, bias=1.0)
                k.on("scalar", sg1)

                def vg2(v):
                    A2 = C.acc[0:4, 0:2048]
                    v.wait("S")
                    v.scalar_tensor_tensor(A2, C.A1, 0.0, A2, OP.min, OP.subtract)
                    v.wait("V")
                    v.tensor_tensor_scan(C.A1, A2, A2, 0.0, OP.add, OP.bypass)
                    v.wait("V")
                    v.tensor_tensor(C.A0, C.A0, C.A1, OP.subtract)
                    v.wait("V")
                    v.tensor_tensor_scan(A2, C.A0, C.A0, -3.0e38, OP.max, OP.bypass)
                    v.wait("V")
                    v.tensor_tensor(C.A1, C.A1, A2, OP.add)
                k.on("vector", vg2)

                def sg2(s):
                    s.wait("V")
                    s.activation(out=C.A1, in_=C.A1, func=AF.Exp, scale=-1.0)
                k.on("scalar", sg2)
                k.barrier()

                def ymv(y):
                    k.inc(y.dma_start(out=stgm_t[32:36, 0:2048], in_=C.acc[0:4, 0:2048]), "dY")
                    k.inc(y.dma_start(out=stgm_t[64:68, 0:2048], in_=C.A1), "dY")
                k.on("sync", ymv)
                k.barrier()

                if True:
                    pgc = pxm_t[:, 0:64]

                    def tgc(t):
                        for tc in range(16):
                            t.transpose(pgc[:, tc * 4:(tc + 1) * 4], C.A0[:, tc * 128:(tc + 1) * 128],
                                        idstage[0:4, 0:4])
                    k.on("tensor", tgc)

                    def vgc(v):
                        v.wait("T")
                        v.tensor_copy(C.gcol[:, :], pgc[:, :])
                    k.on("vector", vgc)
                    k.barrier()

                p2s.close()
                with ExitStack() as p4s:
                    en = p4s.enter_context
                    vh_t = en(nc.sbuf_tensor("vh", [128, 16 * 512], BF16))
                    mb5_t = en(nc.sbuf_tensor("mb5", [128, 512], F32))
                    onc_t = en(nc.sbuf_tensor("onc", [128, 8], BF16))
                    Et5_t = en(nc.sbuf_tensor("Et5", [128, 2 * 512], BF16))
                    Ct5_t = en(nc.sbuf_tensor("Ct5", [128, 2 * 512], BF16))
                    trilb_t = en(nc.sbuf_tensor("trilb", [128, 4 * 512], BF16))
                    nrb_t = en(nc.sbuf_tensor("nrb", [128, 512], F32))
                    cnb_t = en(nc.sbuf_tensor("cnb", [128, 4 * 512], BF16))
                    lsk_t = en(nc.sbuf_tensor("lsk", [128, 2 * 2048], BF16))
                    xzs_t = en(nc.sbuf_tensor("xzs", [128, 2 * 2048], BF16))
                    hsT_t = en(nc.sbuf_tensor("hsT", [128, 16 * 512], BF16))
                    dwn_t = en(nc.sbuf_tensor("dwn", [128, 16 * 512], BF16))
                    st5_t = en(nc.sbuf_tensor("st5", [128, 16], F32))
                    stgd_t = en(nc.sbuf_tensor("stgd", [128, 2 * 512], F32))
                    pqk5_t = en(nc.psum_tensor("pqk5", [128, 2 * 512], F32))
                    pcx5_t = en(nc.psum_tensor("pcx5", [128, 4 * 512], F32))
                    prs5_t = en(nc.psum_tensor("prs5", [1, 512], F32))
                    ptx5_t = en(nc.psum_tensor("ptx5", [128, 1024], BF16))
                    C.qh = xlnT_t[:, 0:8192]
                    C.kh = xlnT_t[:, 8192:16384]
                    C.vh, C.mb5, C.onc = vh_t[:, :], mb5_t[:, :], onc_t[:, :]
                    C.mr5 = stgm_t[0:1, 0:2048]
                    C.er5 = stgm_t[0:1, 2048:4096]
                    C.msrc = stgm_t[32:36, 0:2048]
                    C.esrc = stgm_t[64:68, 0:2048]
                    nrm_t = en(nc.sbuf_tensor("nrm", [1, 512], F32))
                    C.nrm = nrm_t[:, :]
                    C.Et5, C.Ct5, C.trilb, C.nrb = (Et5_t[:, :], Ct5_t[:, :], trilb_t[:, :], nrb_t[:, :])
                    C.cnb, C.hts, C.lsk, C.xzs, C.hsT, C.hsT_t = (cnb_t[:, :], xtile[:, :], lsk_t[:, :],
                                                                  xzs_t[:, :], hsT_t[:, :], hsT_t)
                    C.dwn, C.st5, C.stgd = dwn_t[:, :], st5_t[:, :], stgd_t[:, :]
                    C.pqk5, C.pcx5, C.prs5, C.ptx5 = pqk5_t[:, :], pcx5_t[:, :], prs5_t[:, :], ptx5_t[:, :]
                    C.tdwc = {}
                    C.dwv = []
                    lskc = []
                    for b8 in range(8):
                        def tl5(t, b8=b8):
                            if b8 < 2:
                                t.wait("G")
                            if b8 >= 2:
                                t.wait("V", lskc[b8 - 2])
                            t.matmul(C.pqk5[:, (b8 % 2) * 512:(b8 % 2) * 512 + 512], onesb[0:1, 0:128],
                                     C.lsk[0:1, b8 * 512:(b8 + 1) * 512], start=True, stop=True)
                        k.on("tensor", tl5)

                        def vl5(v, b8=b8):
                            v.wait("T")
                            v.tensor_copy(C.lsk[:, b8 * 512:(b8 + 1) * 512],
                                          C.pqk5[:, (b8 % 2) * 512:(b8 % 2) * 512 + 512])
                            lskc.append(k.c["V"])
                        k.on("vector", vl5)
                    k.barrier()
                    load_cast(C.stgd, [(I["tril"][r], (lambda r: lambda c: C.trilb[:, r * 512:(r + 1) * 512])(r),
                                        128, 512) for r in range(4)], per=512)
                    load_rows_cast(C.stgd, [(I["lnw"][0:1, p2 * 512:(p2 + 1) * 512],
                                             C.lsk[0:1, p2 * 512:(p2 + 1) * 512], 512) for p2 in range(4)] +
                                           [(I["skipv"][0:1, p2 * 512:(p2 + 1) * 512],
                                             C.lsk[0:1, 2048 + p2 * 512:2048 + (p2 + 1) * 512], 512)
                                            for p2 in range(4)], per=512)

                    def vo5(v):
                        v.memset(C.onc[:, :], 1.0)
                    k.on("vector", vo5)

                    k.barrier()
                    for sb in range(4):
                        C.hs_done = {}
                        for h in range(4):
                            l1aa_p4_head(sb, h, C)
                        l1aa_p4_down(sb, C)
                    k.barrier()
            aas.close()
            k.barrier()

        def sub_l1ab():
            with nc.sbuf_tensor("yb2", [128, TT * 1024], BF16) as y2_t:
                y2 = y2_t[:, :]
                with nc.psum_tensor("psb", [128, 2048], BF16) as psb:
                    emit_ln(psb[:, :])
                k.barrier()
                park_R()
                with nc.sbuf_tensor("qta", [128, 16 * 1040], BF16) as qta_t, \
                     nc.sbuf_tensor("wqp", [128, 512], BF16) as wqp_t, \
                     nc.sbuf_tensor("wqs", [128, 512], F32) as wqs_t, \
                     nc.sbuf_tensor("mtk", [128, 260], F32) as mtk_t, \
                     nc.sbuf_tensor("mtb", [128, 256], BF16) as mtb_t, \
                     nc.sbuf_tensor("mpb", [1, 2 * 2048], BF16) as mpb_t, \
                     nc.sbuf_tensor("negc", [1, 128], BF16) as negc_t:
                    qta, wqp, wqs = qta_t[:, :], wqp_t[:, :], wqs_t[:, :]
                    mtk, mtb, mpb, negc = mtk_t[:, :], mtb_t[:, :], mpb_t[:, :], negc_t[:, :]
                    qfm = xlnT

                    def yw(y):
                        k.inc(y.dma_start(out=wqs[:, :], in_=I["l1WqPk"][:, :]), "dY")
                    k.on("sync", yw)

                    def gw(g):
                        g.wait("dY")
                        g.tensor_copy(wqp[:, :], wqs[:, :])
                    k.on("gpsimd", gw)

                    def vn(v):
                        v.memset(negc[:, :], -1.0)
                        ap = qta_t.ap().rearrange("p (a b) -> p a b", a=16)[:, :, 64::65]
                        v.memset(ap, 1.0)
                    k.on("vector", vn)
                    k.barrier()

                    qcnt, qec = {}, {}
                    with nc.psum_tensor("pq2", [128, 2 * 512], F32) as pq2_t:
                        pq2 = pq2_t[:, :]
                        for h in range(16):
                            po, j = 64 * (h % 2), h // 2
                            for sb in range(4):
                                def tq(t, h=h, po=po, j=j, sb=sb):
                                    u = h * 4 + sb
                                    if u >= 2:
                                        t.wait("V", qec[u - 2])
                                    t.matmul(pq2[po:po + 64, (u % 2) * 512:(u % 2) * 512 + 512],
                                             wqp[po:po + 64, j * 64:j * 64 + 64],
                                             qfm[po:po + 64, j * 2048 + sb * 512: j * 2048 + sb * 512 + 512],
                                             start=True, stop=True, tile_position=(po, po))
                                    qcnt[u] = k.c["T"]
                                k.on("tensor", tq)

                                def vq3(v, h=h, po=po, j=j, sb=sb):
                                    u = h * 4 + sb
                                    v.wait("T", qcnt[u])
                                    v.tensor_copy(qfm[po:po + 64, j * 2048 + sb * 512: j * 2048 + sb * 512 + 512],
                                                  pq2[po:po + 64, (u % 2) * 512:(u % 2) * 512 + 512])
                                    qec[u] = k.c["V"]
                                k.on("vector", vq3)
                    k.barrier()

                    with nc.psum_tensor("ptp3", [128, 2048], BF16) as ptp3_t:
                        ptp3 = ptp3_t[:, :]
                        tc3, ec3 = [], []
                        for i in range(TT):
                            def tt3(t, i=i):
                                if i >= 2:
                                    t.wait("V", ec3[i - 2])
                                for c in range(FC):
                                    t.transpose(ptp3[:, (i % 2) * 1024 + c * 128:(i % 2) * 1024 + c * 128 + 128],
                                                qfm[:, c * 2048 + i * 128: c * 2048 + i * 128 + 128],
                                                identb[:, :])
                                tc3.append(k.c["T"])
                            k.on("tensor", tt3)

                            def ve3(v, i=i):
                                v.wait("T", tc3[i])
                                src = ptp3[:, (i % 2) * 1024:(i % 2) * 1024 + 1024].rearrange(
                                    "p (c a b) -> p c a b", c=FC, a=2)
                                dst = qta_t.ap().rearrange("p (a b) -> p a b", a=16)[
                                    :, i:i + 1, :].rearrange("p a (c q e) -> p a c q e", c=FC, q=2)[:, 0, :, :, 0:64]
                                v.tensor_copy(dst, src)
                                ec3.append(k.c["V"])
                            k.on("vector", ve3)
                    k.barrier()

                    with nc.psum_tensor("pm1", [128, 2 * 1024], F32) as pm1_t:
                        pm1 = pm1_t[:, :]
                        mc, mv = {}, {}
                        for h in range(16):
                            po, j = 64 * (h % 2), h // 2
                            for qt in range(16):
                                for half in range(2):
                                    u = (h * 16 + qt) * 2 + half

                                    def tm1(t, h=h, po=po, j=j, qt=qt, half=half, u=u):
                                        if u >= 2:
                                            t.wait("V", mv[u - 2])
                                        for kb in range(2):
                                            t.matmul(pm1[:, (u % 2) * 1024 + kb * 512:(u % 2) * 1024 + kb * 512 + 512],
                                                     qfm[po:po + 64, j * 2048 + qt * 128: j * 2048 + qt * 128 + 128],
                                                     qfm[po:po + 64, j * 2048 + half * 1024 + kb * 512: j * 2048 + half * 1024 + kb * 512 + 512],
                                                     start=True, stop=True, tile_position=(po, 0))
                                        mc[u] = k.c["T"]
                                    k.on("tensor", tm1)

                                    def vm1(v, h=h, qt=qt, half=half, u=u):
                                        v.wait("T", mc[u])
                                        v.wait("V")
                                        v.reduce_max(mtk[:, 256:257] if half == 0 else mtk[:, 257:258],
                                                     pm1[:, (u % 2) * 1024:(u % 2) * 1024 + 1024], axis=AX)
                                        if half == 1:
                                            v.wait("V")
                                            v.tensor_tensor(mtk[:, h * 16 + qt:h * 16 + qt + 1],
                                                            mtk[:, 256:257], mtk[:, 257:258], OP.max)
                                        mv[u] = k.c["V"]
                                    k.on("vector", vm1)
                        def vmb(v):
                            v.wait("V")
                            v.tensor_copy(mtb[:, 0:256], mtk[:, 0:256])
                        k.on("vector", vmb)
                        k.barrier()
                        with nc.psum_tensor("ptm", [128, 256], BF16) as ptm_t:
                            ptm = ptm_t[:, :]

                            def tmt(t):
                                for b in range(2):
                                    t.transpose(ptm[:, b * 128:(b + 1) * 128], mtb[:, b * 128:(b + 1) * 128],
                                                identb[:, :])
                            k.on("tensor", tmt)

                            def vmt(v):
                                v.wait("T")
                                v.tensor_copy(mtb[:, 0:256], ptm[:, 0:256])
                            k.on("vector", vmt)
                            k.barrier()

                            def ym2(y):
                                dv = d_m.rearrange("h (q p) -> (h q) p", q=16)
                                for b in range(2):
                                    k.inc(y.dma_start(out=dv[b * 128:(b + 1) * 128, :],
                                                      in_=mtb[:, b * 128:(b + 1) * 128]), "dY")
                            k.on("sync", ym2)
                            k.barrier()

                    with nc.sbuf_tensor("wpp", [128, 16 * 1024], BF16) as wpp_t, \
                         nc.sbuf_tensor("bpr", [1, 1024], BF16) as bpr_t, \
                         nc.sbuf_tensor("Pt2", [128, 2 * 2048], BF16) as Pt2_t, \
                         nc.sbuf_tensor("cs2", [128, 16 * 512], BF16) as cs2_t, \
                         nc.sbuf_tensor("ls2", [128, 512], F32) as ls2_t, \
                         nc.sbuf_tensor("lb2", [128, 512], F32) as lb2_t, \
                         nc.sbuf_tensor("stg3", [128, 2 * 2048], F32) as stg3_t, \
                         nc.psum_tensor("pS2", [128, 2048], F32) as pS2_t, \
                         nc.psum_tensor("pC2", [128, 2 * 512], F32) as pC2_t, \
                         nc.psum_tensor("pO2", [128, 2 * 512], F32) as pO2_t:
                        wpp, bpr, Pt2, cs2, ls2, lb2 = (wpp_t[:, :], bpr_t[:, :], Pt2_t[:, :],
                                                        cs2_t[:, :], ls2_t[:, :], lb2_t[:, :])
                        pS2, pC2, pO2 = pS2_t[:, :], pC2_t[:, :], pO2_t[:, :]
                        load_cast(stg3_t[:, :],
                                  [(I["l1WpPk"][:, cb * 2048:(cb + 1) * 2048],
                                    (lambda cb: lambda c: wpp[0:64, cb * 2048:(cb + 1) * 2048])(cb), 64, 2048)
                                   for cb in range(8)])
                        load_rows_cast(stg3_t[:, :], [(I["l1bp"][0:1, :], bpr[0:1, 0:1024], 1024)])
                        k.barrier()

                        scnt2, pcnt2, vcnt2, ccnt, gcnt2, dmac, mdc = {}, {}, {}, {}, {}, {}, {}
                        ocnt, expseq, pvseq = [], [], []
                        for qb in range(4):
                            for h in range(16):
                                po = 64 * (h % 2)
                                j = h // 2

                                def ymh(y, qb=qb, h=h):
                                    hh = qb * 16 + h
                                    if hh >= 2:
                                        y.wait("T", pcnt2[((hh - 2) // 16, (hh - 2) % 16, 3)])
                                    k.inc(y.dma_start(out=mpb[0:1, (h % 2) * 2048:(h % 2) * 2048 + 2048],
                                                      in_=d_m[h:h + 1, :]), "dY%d" % (h % 2))
                                    mdc[(qb, h)] = k.c["dY%d" % (h % 2)]
                                k.on("sync", ymh)

                                for kg in range(4):
                                    gidx = (qb * 16 + h) * 4 + kg

                                    def tS(t, qb=qb, h=h, po=po, j=j, kg=kg, gidx=gidx):
                                        if gidx >= 1:
                                            t.wait("S", expseq[gidx - 1])
                                        if kg == 0:
                                            t.wait("dY%d" % (h % 2), mdc[(qb, h)])
                                        for kc in range(4 * kg, 4 * kg + 4):
                                            t.matmul(pS2[:, (kc % 4) * 512:(kc % 4) * 512 + 512],
                                                     qfm[po:po + 64, j * 2048 + kc * 128: j * 2048 + kc * 128 + 128],
                                                     qfm[po:po + 64, j * 2048 + qb * 512: j * 2048 + qb * 512 + 512],
                                                     start=True, stop=False, tile_position=(po, 0))
                                            t.matmul(pS2[:, (kc % 4) * 512:(kc % 4) * 512 + 512],
                                                     negc[0:1, 0:128],
                                                     mpb[0:1, (h % 2) * 2048 + qb * 512:(h % 2) * 2048 + qb * 512 + 512],
                                                     start=False, stop=True)
                                        pcnt2[(qb, h, kg)] = k.c["T"]
                                    k.on("tensor", tS)

                                    def sE(s, qb=qb, h=h, kg=kg, gidx=gidx):
                                        s.wait("T", pcnt2[(qb, h, kg)])
                                        if gidx >= 2:
                                            s.wait("T", pvseq[gidx - 2])
                                        s.activation(out=Pt2[:, (gidx % 2) * 2048:(gidx % 2) * 2048 + 2048],
                                                     in_=pS2[:, :], func=AF.Exp, scale=8.0)
                                        scnt2[(qb, h, kg)] = k.c["S"]
                                        expseq.append(k.c["S"])
                                    k.on("scalar", sE)

                                    def tPV(t, qb=qb, h=h, kg=kg, gidx=gidx):
                                        t.wait("S", scnt2[(qb, h, kg)])
                                        if kg == 0 and qb * 16 + h >= 2:
                                            hh = qb * 16 + h - 2
                                            t.wait("V", ccnt[(hh // 16, hh % 16)])
                                        for kc in range(4 * kg, 4 * kg + 4):
                                            t.matmul(pC2[0:65, (h % 2) * 512:(h % 2) * 512 + 512],
                                                     qta[:, kc * 1040 + h * 65: kc * 1040 + h * 65 + 65],
                                                     Pt2[:, (gidx % 2) * 2048 + (kc % 4) * 512:(gidx % 2) * 2048 + (kc % 4) * 512 + 512],
                                                     start=(kg == 0 and kc == 0), stop=(kg == 3 and kc == 4 * kg + 3))
                                        pcnt2[(qb, h, "pv", kg)] = k.c["T"]
                                        pvseq.append(k.c["T"])
                                    k.on("tensor", tPV)

                                def vL(v, qb=qb, h=h):
                                    v.wait("T", pcnt2[(qb, h, "pv", 3)])
                                    hh = qb * 16 + h - 1
                                    if hh >= 0 and (hh // 16, hh % 16) in dmac:
                                        v.wait("dA%d" % ((hh % 16) % 2), dmac[(hh // 16, hh % 16)])
                                    v.wait("V")
                                    v.tensor_copy(ls2[64:65, 0:512], pC2[64:65, (h % 2) * 512:(h % 2) * 512 + 512])
                                    vcnt2[(qb, h, "l")] = k.c["V"]
                                k.on("vector", vL)

                                def yL(y, qb=qb, h=h):
                                    y.wait("V", vcnt2[(qb, h, "l")])
                                    k.inc(y.dma_start(out=ls2[0:1, 0:512], in_=ls2[64:65, 0:512]), "dA%d" % (h % 2))
                                    dmac[(qb, h)] = k.c["dA%d" % (h % 2)]
                                k.on("sync", yL)

                                def vR(v, qb=qb, h=h):
                                    v.wait("dA%d" % (h % 2), dmac[(qb, h)])
                                    v.reciprocal(ls2[0:1, 0:512], ls2[0:1, 0:512])
                                    vcnt2[(qb, h, "r")] = k.c["V"]
                                k.on("vector", vR)

                                def gB(t, qb=qb, h=h):
                                    t.wait("V", vcnt2[(qb, h, "r")])
                                    if qb * 16 + h >= 1:
                                        t.wait("V")
                                    t.matmul(pO2[0:64, 0:512], onesf[0:1, 0:64], ls2[0:1, 0:512],
                                             start=True, stop=True)
                                    gcnt2[(qb, h)] = k.c["T"]
                                k.on("tensor", gB)

                                def vB(v, qb=qb, h=h):
                                    v.wait("T", gcnt2[(qb, h)])
                                    v.tensor_copy(lb2[0:64, 0:512], pO2[0:64, 0:512])
                                    gcnt2[(qb, h, "c")] = k.c["V"]
                                k.on("vector", vB)

                                def vC(v, qb=qb, h=h):
                                    v.wait("V", gcnt2[(qb, h, "c")])
                                    if h == 0 and qb > 0:
                                        v.wait("T", pcnt2[(qb - 1, "o", 7)])
                                    v.tensor_tensor(cs2[0:64, h * 512:(h + 1) * 512],
                                                    pC2[0:64, (h % 2) * 512:(h % 2) * 512 + 512],
                                                    lb2[0:64, 0:512], OP.mult)
                                    ccnt[(qb, h)] = k.c["V"]
                                k.on("vector", vC)

                            for uo in range(8):
                                sc, nb = uo // 2, uo % 2

                                def tO(t, qb=qb, uo=uo, sc=sc, nb=nb):
                                    t.wait("V", ccnt[(qb, 15)])
                                    g = qb * 8 + uo
                                    if g >= 2:
                                        t.wait("V", ocnt[g - 2])
                                    bank = pO2[:, (uo % 2) * 512:(uo % 2) * 512 + 512]
                                    for h in range(16):
                                        t.matmul(bank, cs2[0:64, h * 512 + sc * 128: h * 512 + sc * 128 + 128],
                                                 wpp[0:64, h * 1024 + nb * 512: h * 1024 + nb * 512 + 512],
                                                 start=(h == 0), stop=False)
                                    t.matmul(bank, onesb[0:1, 0:128], bpr[0:1, nb * 512: nb * 512 + 512],
                                             start=False, stop=True)
                                    pcnt2[(qb, "o", uo)] = k.c["T"]
                                k.on("tensor", tO)

                                def vO(v, qb=qb, uo=uo, sc=sc, nb=nb):
                                    v.wait("T", pcnt2[(qb, "o", uo)])
                                    bank = pO2[:, (uo % 2) * 512:(uo % 2) * 512 + 512]
                                    v.tensor_copy(y2[:, (qb * 4 + sc) * 1024 + nb * 512:(qb * 4 + sc) * 1024 + nb * 512 + 512], bank)
                                    ocnt.append(k.c["V"])
                                k.on("vector", vO)
                        k.barrier()
                unpark_R()

                def vres2(v):
                    for i in range(TT):
                        v.tensor_tensor(R[:, i * 1024:(i + 1) * 1024], R[:, i * 1024:(i + 1) * 1024],
                                        y2[:, i * 1024:(i + 1) * 1024], OP.add)
                k.on("vector", vres2)
                k.barrier()

        subs = [sub_l0aa, sub_l0ffn, sub_l0ab, sub_l0ffn, sub_l1aa, sub_l1ffn, sub_l1ab, sub_l1ffn]
        for f in subs[:N_SUB]:
            if f is not None:
                f()
        final_out()
    return nc


def _rope_tables():
    scale = np.arange(0, 64, 2, dtype=np.float32)
    with np.errstate(over="ignore"):
        p = np.float32(10000.0) ** scale
    freqs = (np.float32(1.0) / p).astype(np.float32)
    ang = (np.arange(S, dtype=np.float32)[:, None] * freqs[None, :]).astype(np.float32)
    Ac, As = np.cos(ang).astype(np.float32), np.sin(ang).astype(np.float32)
    fb = (np.float32(10000.0) ** (np.arange(0, 64, 2, dtype=np.float32) / np.float32(64.0))).astype(np.float32)
    angb = (np.arange(S, dtype=np.float32)[:, None] / fb[None, :]).astype(np.float32)
    Bc, Bs = np.cos(angb).astype(np.float32), np.sin(angb).astype(np.float32)
    rep = lambda a: np.ascontiguousarray(np.tile(a, (1, 16)).astype(np.float32))
    return rep(Ac), rep(As), rep(Bc), rep(Bs)


def _wqpk(Wq):
    arr = np.zeros((128, 512), dtype=np.float32)
    for h in range(16):
        arr[(h % 2) * 64:(h % 2) * 64 + 64, (h // 2) * 64:(h // 2) * 64 + 64] = Wq[h].T
    return np.ascontiguousarray(arr)


def _host_inputs(inp):
    f32 = lambda a: np.ascontiguousarray(np.asarray(a, dtype=np.float32))
    Ac, As, Bc, Bs = _rope_tables()
    tril = np.zeros((4, 128, 512), dtype=np.float32)
    for r in range(4):
        for p in range(128):
            tril[r, p, 128 * r + p:] = 1.0
    bd = {}
    for nm, w in (("bdq", inp["L1_attn_a_w_q"]), ("bdk", inp["L1_attn_a_w_k"]), ("bdv", inp["L1_attn_a_w_v"])):
        w = f32(w)
        m = np.zeros((16, 128, 128), dtype=np.float32)
        for c in range(16):
            for b in range(32):
                m[c, 4 * b:4 * b + 4, 4 * b:4 * b + 4] = w[32 * c + b].T
        bd[nm] = m
    return {
        "aaWq": f32(inp["L0_attn_a_Wq"]), "aaWk": f32(inp["L0_attn_a_Wk"]),
        "aaWv": f32(inp["L0_attn_a_Wv"]), "aaWo": f32(inp["L0_attn_a_Wo"]),
        "aab": np.ascontiguousarray(np.stack([f32(inp["L0_attn_a_b" + c]) for c in "qkvo"], 0)),
        "abWq": np.ascontiguousarray(f32(inp["L0_attn_b_wq"]).T),
        "abWk": np.ascontiguousarray(f32(inp["L0_attn_b_wk"]).T),
        "abWv": np.ascontiguousarray(f32(inp["L0_attn_b_wv"]).T),
        "abWo": np.ascontiguousarray(f32(inp["L0_attn_b_wo"]).T),
        "abb": np.ascontiguousarray(np.stack([f32(inp["L0_attn_b_b" + c]) for c in "qkvo"], 0)),
        "abWoPk": np.ascontiguousarray(
            f32(inp["L0_attn_b_wo"]).T.reshape(16, 64, 1024).transpose(1, 0, 2).reshape(64, 16 * 1024)),
        "f0k": np.ascontiguousarray(np.asarray(inp["L0_ffn_kernel"], dtype=ml_dtypes.bfloat16)),
        "f0scales": np.ascontiguousarray(
            np.concatenate([f32(inp["L0_ffn_input_scale"]), f32(inp["L0_ffn_kernel_scale"])]).reshape(1, 2)),
        "f0bias": np.ascontiguousarray(np.asarray(inp["L0_ffn_bias"], dtype=ml_dtypes.bfloat16).reshape(1, D)),
        "ident": np.eye(128, dtype=np.float32),
        "ropeAc": Ac, "ropeAs": As, "ropeBc": Bc, "ropeBs": Bs,
        "upT": np.ascontiguousarray(f32(inp["L1_attn_a_w_up"]).T),
        "convw": np.ascontiguousarray(f32(inp["L1_attn_a_conv_w"]).reshape(2048, 4)),
        "convb": np.ascontiguousarray(f32(inp["L1_attn_a_conv_b"]).reshape(2048, 1)),
        "bdq": bd["bdq"], "bdk": bd["bdk"], "bdv": bd["bdv"],
        "wigfgT": np.ascontiguousarray(
            np.concatenate([f32(inp["L1_attn_a_w_ig"]).T, f32(inp["L1_attn_a_w_fg"]).T], 1)),
        "bigfg": np.ascontiguousarray(
            np.stack([f32(inp["L1_attn_a_b_ig"]), f32(inp["L1_attn_a_b_fg"])], 1)),
        "lnw": np.ascontiguousarray(f32(inp["L1_attn_a_ln_w"]).reshape(1, 2048)),
        "skipv": np.ascontiguousarray(f32(inp["L1_attn_a_skip"]).reshape(1, 2048)),
        "downT": np.ascontiguousarray(f32(inp["L1_attn_a_w_down"]).T),
        "l1Wq": np.ascontiguousarray(f32(inp["L1_attn_b_Wq"]).transpose(0, 2, 1)),
        "l1WqPk": _wqpk(f32(inp["L1_attn_b_Wq"])),
        "l1WpPk": np.ascontiguousarray(
            f32(inp["L1_attn_b_Wp"]).T.reshape(16, 64, 1024).transpose(1, 0, 2).reshape(64, 16 * 1024)),
        "l1Wp": np.ascontiguousarray(f32(inp["L1_attn_b_Wp"]).T),
        "l1bp": np.ascontiguousarray(f32(inp["L1_attn_b_bp"]).reshape(1, D)),
        "w1T": np.ascontiguousarray(f32(inp["L1_ffn_w1"]).T),
        "w1b": np.ascontiguousarray(f32(inp["L1_ffn_b1"]).reshape(1, 4096)),
        "w2T": np.ascontiguousarray(f32(inp["L1_ffn_w2"]).T),
        "w2b": np.ascontiguousarray(f32(inp["L1_ffn_b2"]).reshape(1, D)),
        "tril": tril,
    }


def kernel(**inp):
    shared = _host_inputs(inp)
    x = np.ascontiguousarray(np.asarray(inp["L0_attn_a_x"], dtype=np.float32))
    in_maps = [dict(shared, x=np.ascontiguousarray(x[b])) for b in range(8)]
    nc = build_program()
    res = run_bass_kernel_spmd(nc, in_maps, core_ids=list(range(8)))
    return np.stack([r["out"] for r in res.results], 0)
